# version 46 (final)
# baseline (speedup 1.0000x reference)
import numpy as np
import ml_dtypes
import concourse.bass as bass
import concourse.mybir as mybir
from concourse.bass_utils import run_bass_kernel_spmd
from concourse.bass_core_rust import IndirectOffsetOnAxis

F32 = mybir.dt.float32
BF16 = mybir.dt.bfloat16
AF = mybir.ActivationFunctionType
ALU = mybir.AluOpType
AX = mybir.AxisListType

NCORES = 8
NB = 4
NPASS = NB * 2
DC = 8
TT = 8
NH = 16
E = 16
EPS = 1e-5
SM_BIAS = -3.0


class Ctr:
    def __init__(self, nc, name):
        self.h = nc.alloc_semaphore(name)
        self.n = 0


def build():
    nc = bass.Bass("TRN2", target_bir_lowering=False)

    xin = nc.dram_tensor("xp", [NB, 2, 128, TT * 1024], F32, kind="ExternalInput").ap()
    wqkvo = nc.dram_tensor("wqkvop", [4, 128, 8192], F32, kind="ExternalInput").ap()
    wg = nc.dram_tensor("wgp", [128, 128], F32, kind="ExternalInput").ap()
    w1 = nc.dram_tensor("w1p", [E, 128, 8192], F32, kind="ExternalInput").ap()
    w2 = nc.dram_tensor("w2p", [E, 128, 4096], F32, kind="ExternalInput").ap()
    ws1 = nc.dram_tensor("ws1p", [128, 16384], F32, kind="ExternalInput").ap()
    ws2 = nc.dram_tensor("ws2p", [128, 8192], F32, kind="ExternalInput").ap()
    identin = nc.dram_tensor("identc", [128, 128], BF16, kind="ExternalInput").ap()
    diagin = nc.dram_tensor("diagc", [128, 128], BF16, kind="ExternalInput").ap()
    ones64in = nc.dram_tensor("ones64c", [1, 64], BF16, kind="ExternalInput").ap()
    ultriin = nc.dram_tensor("ultric", [128, 128], BF16, kind="ExternalInput").ap()
    iota16in = nc.dram_tensor("iota16c", [128, 16], F32, kind="ExternalInput").ap()
    iota128in = nc.dram_tensor("iota128c", [128, 32], F32, kind="ExternalInput").ap()
    pos128in = nc.dram_tensor("pos128c", [128, 1], F32, kind="ExternalInput").ap()
    onesf1in = nc.dram_tensor("onesf1c", [1, 128], F32, kind="ExternalInput").ap()
    ones1b = nc.dram_tensor("ones1bc", [128, 1], BF16, kind="ExternalInput").ap()
    tokp16in = nc.dram_tensor("tokp16c", [128, 16], F32, kind="ExternalInput").ap()
    NBLK = 32
    NSLOT = NBLK * 128
    tokd = nc.dram_tensor("tokd", [NSLOT, 16], F32, kind="Internal").ap()
    pdram = nc.dram_tensor("pdram", [NSLOT, 1024], BF16, kind="Internal").ap()
    ln3d = nc.dram_tensor("ln3d", [1024, 1024], BF16, kind="Internal").ap()
    w1m = nc.dram_tensor("w1m", [E * 128, 8192], BF16, kind="Internal").ap()
    w2m = nc.dram_tensor("w2m", [E * 128, 4096], BF16, kind="Internal").ap()
    wqm = nc.dram_tensor("wqm", [4, 128, 8192], BF16, kind="Internal").ap()
    ws1m = nc.dram_tensor("ws1m", [128, 16384], BF16, kind="Internal").ap()
    ws2m = nc.dram_tensor("ws2m", [128, 8192], BF16, kind="Internal").ap()
    yout = nc.dram_tensor("yp", [NB, 2, 128, TT * 1024], F32, kind="ExternalOutput").ap()

    A = nc.alloc_sbuf_tensor
    xtok = A("xtok", [128, TT, 1024], F32)
    lnfm = A("lnfm", [128, DC, 1024], BF16)
    kfm = A("kfm", [128, DC, 1024], BF16)
    vaug = A("vaug", [128, TT, 1040], BF16)
    wbuf = A("wbuf", [128, 2, DC, 1024], BF16)
    w2buf = A("w2buf", [128, 4, 1024], BF16)
    wgb = A("wgb", [128, DC, 16], BF16)
    zb = A("zb", [128, TT, 1024], F32)
    lntok = A("lntok", [128, 2, 1024], BF16)
    t32 = A("t32", [128, 1024], F32)
    arena = A("arena", [128, 20480], BF16)
    aap = arena.ap()
    sgt = lntok.ap()

    def carve(off_bytes, c, f):
        return aap[:, off_bytes // 2:off_bytes // 2 + c * f].rearrange(
            "p (c f) -> p c f", c=c)

    qfm = carve(0, DC, 1024)
    ofm = carve(16384, DC, 1024)
    Eb = carve(32768, 2, 1024)
    Rb = carve(36864, 1, 1024)[0:64, 0, :]
    rb = carve(38912, 1, 1024)[0:1, 0, :]
    ws2b = carve(0, DC, 1024)
    ysp = carve(16384, DC, 1024)
    lnflat = lnfm.ap().rearrange("p c f -> p (c f)")
    xfm = lnflat[:, 0:2048].rearrange("p (b c f) -> p b c f", b=2, c=8)
    hpf = lnflat[:, 2048:2560]
    hpTs = lnflat[:, 7168:7680]
    lnf32 = lnfm.bitcast(F32)
    psc = lnflat[:, 3072:5120].rearrange("p (b f) -> p b f", b=2)
    xg2 = carve(32768, 2, 1024)
    arena32 = arena.bitcast(F32)
    gb0 = aap[:, 16384:17408]
    gb1 = aap[:, 18432:19456]
    zb_bf_stage = zb.bitcast(BF16).ap().rearrange(
        "p t f -> p (t f)")

    identb = A("identb", [128, 128], BF16)
    ultrib = A("ultrib", [128, 128], BF16)
    iota16b = A("iota16b", [128, 16], F32)
    iota128b = A("iota128b", [128, 32], F32)
    pos128b = A("pos128b", [128, 1], F32)
    onesf1b = A("onesf1b", [1, 128], F32)
    ones1bb = A("ones1bb", [128, 1], BF16)
    indbf = A("indbf", [128, 16], BF16)
    slotf = A("slotf", [128, 16], F32)
    s16a = A("s16a", [128, 16], F32)
    sl2f = A("sl2f", [128, 2], F32)
    slot2i = A("slot2i", [128, TT, 2], mybir.dt.int32)
    prefb = A("prefb", [128, TT, 16], F32)
    selp = A("selp", [128, TT, 2], F32)
    cwvsb = A("cwvsb", [128, TT, 2], F32)
    cnts_sb = A("cnts_sb", [1, 128], F32)
    run_sb = A("run_sb", [1, 16], F32)
    tbase_sb = A("tbase_sb", [1, 128], F32)
    rbase_sb = A("rbase_sb", [1, 16], F32)
    r128i = A("r128i", [1, 16], mybir.dt.int32)
    r128f = A("r128f", [1, 16], F32)
    r128g = A("r128g", [1, 16], F32)
    r128h = A("r128h", [1, 16], F32)
    blkrow_sb = A("blkrow_sb", [1, 32], F32)
    rbaseb = A("rbaseb", [128, 16], F32)
    idxwf = A("idxwf", [128, NBLK], F32)
    idxwi = A("idxwi", [128, NBLK], mybir.dt.int32)
    tib = A("tib", [128, 2, 16], F32)
    tii = A("tii", [128, 2, 1], mybir.dt.int32)
    tidv = A("tidv", [128, 16], F32)
    tokp16b = A("tokp16b", [128, 16], F32)
    zt16 = A("zt16", [128, 16], F32)
    diagb = A("diagb", [128, 128], BF16)
    ones64b = A("ones64b", [1, 64], BF16)
    st12 = A("st12", [128, 24], F32)
    mvs = A("mvs", [128, 4], F32)
    aux = A("aux", [128, 24], F32)
    gex = A("gex", [128, 16], F32)
    gsc = A("gsc", [128, 16], F32)
    gtmp = A("gtmp", [128, 16], F32)
    cbias = A("cbias", [128, 1], F32)
    cscale = A("cscale", [128, 1], F32)
    ceps = A("ceps", [128, 1], F32)

    psA = nc.alloc_psum_tensor("psA", [128, 2, 1024], F32)
    psB = nc.alloc_psum_tensor("psB", [128, 2, 1024], F32)
    psB_bf = psB.bitcast(BF16)
    psA_bf = psA.bitcast(BF16)

    sW = Ctr(nc, "sW")
    sE = Ctr(nc, "sE")
    sX = Ctr(nc, "sX")
    sT = Ctr(nc, "sT")
    sS = Ctr(nc, "sS")
    sV = Ctr(nc, "sV")
    sBar = Ctr(nc, "sBar")

    EVN = ("te", "se", "ve", "po", "sy")
    events = {k: [] for k in EVN}

    def on(eng, fn):
        events[eng].append(fn)

    nbar = [0]

    def barrier():
        nbar[0] += 1
        target = 5 * nbar[0]
        for k in EVN:
            def f(eng, target=target):
                eng.sem_inc(sBar.h, 1)
                eng.wait_ge(sBar.h, target)
            on(k, f)

    def _waits(eng, waits):
        for (sem, val) in waits:
            if val > 0:
                eng.wait_ge(sem.h, val)

    def V(fn, waits=(), lp=None):
        pre = sV.n
        waits = tuple((s, v) for (s, v) in waits
                      if not (s is sV and v <= pre))

        def f(eng, fn=fn, waits=waits, pre=pre, lp=lp):
            eng.wait_ge(sV.h, pre)
            _waits(eng, waits)
            if lp:
                with nc.allow_low_precision(reason=lp):
                    ins = fn(eng)
            else:
                ins = fn(eng)
            ins.then_inc(sV.h, 1)
        on("ve", f)
        sV.n += 1
        return sV.n

    def S(fn, waits=()):
        pre = sS.n
        waits = tuple((s, v) for (s, v) in waits
                      if not (s is sS and v <= pre))

        def f(eng, fn=fn, waits=waits, pre=pre):
            eng.wait_ge(sS.h, pre)
            _waits(eng, waits)
            fn(eng).then_inc(sS.h, 1)
        on("se", f)
        sS.n += 1
        return sS.n

    def T(fn, waits=(), inc=True, lp=None):
        def f(eng, fn=fn, waits=tuple(waits), lp=lp, inc=inc):
            _waits(eng, waits)
            if lp:
                with nc.allow_low_precision(reason=lp):
                    ins = fn(eng)
            else:
                ins = fn(eng)
            if inc:
                ins.then_inc(sT.h, 1)
        on("te", f)
        if inc:
            sT.n += 1
            return sT.n

    def DMA_W(out, in_, waits=()):
        def f(eng, out=out, in_=in_, waits=tuple(waits)):
            _waits(eng, waits)
            eng.dma_start(out=out, in_=in_).then_inc(sW.h, 16)
        on("po", f)
        sW.n += 16
        return sW.n

    def DMA_X(out, in_, waits=()):
        def f(eng, out=out, in_=in_, waits=tuple(waits)):
            _waits(eng, waits)
            eng.dma_start(out=out, in_=in_).then_inc(sX.h, 16)
        on("sy", f)
        sX.n += 16
        return sX.n

    def DMA_XT(out, in_, waits=()):
        def f(eng, out=out, in_=in_, waits=tuple(waits)):
            _waits(eng, waits)
            eng.dma_start_transpose(out, in_).then_inc(sX.h, 16)
        on("sy", f)
        sX.n += 16
        return sX.n

    def DMA_SCAT(out, idx, in_, waits=()):
        def f(eng, out=out, idx=idx, in_=in_, waits=tuple(waits)):
            _waits(eng, waits)
            eng.indirect_dma_start(out, IndirectOffsetOnAxis(idx, 0), in_,
                                   None).then_inc(sW.h, 16)
        on("po", f)
        sW.n += 16
        return sW.n

    def DMA_SE(out, in_, waits=()):
        def f(eng, out=out, in_=in_, waits=tuple(waits)):
            _waits(eng, waits)
            eng.dma_start(out=out, in_=in_).then_inc(sE.h, 16)
        on("se", f)
        sE.n += 16
        return sE.n

    def DMA_G(out, in_, idx, waits=()):
        def f(eng, out=out, idx=idx, in_=in_, waits=tuple(waits)):
            _waits(eng, waits)
            eng.indirect_dma_start(out, None, in_,
                                   IndirectOffsetOnAxis(idx, 0)).then_inc(sW.h, 16)
        on("po", f)
        sW.n += 16
        return sW.n

    def DMA_GACC(out, in_, idx, waits=()):
        def f(eng, out=out, idx=idx, in_=in_, waits=tuple(waits)):
            _waits(eng, waits)
            eng.indirect_dma_start(out, None, in_,
                                   IndirectOffsetOnAxis(idx, 0),
                                   compute_op=ALU.add).then_inc(sW.h, 16)
        on("po", f)
        sW.n += 16
        return sW.n


    DMA_X(identb.ap()[:], identin[:])
    DMA_X(diagb.ap()[:], diagin[:])
    DMA_X(ones64b.ap()[:], ones64in[:])
    wg_ld = DMA_W(wgb.ap()[:], wg.rearrange("p (c e) -> p c e", e=16))
    DMA_X(ultrib.ap()[:], ultriin[:])
    DMA_X(iota16b.ap()[:], iota16in[:])
    DMA_X(iota128b.ap()[:], iota128in[:])
    DMA_X(pos128b.ap()[:], pos128in[:])
    DMA_X(onesf1b.ap()[:], onesf1in[:])
    cst_x = DMA_X(ones1bb.ap()[:], ones1b[:])
    cst_all = DMA_X(tokp16b.ap()[:], tokp16in[:])
    xld = {
        (0, 0): DMA_W(xtok.ap()[:, 0:4, :].rearrange("p t f -> p (t f)"),
                      xin[0, 0][:, 0:4096]),
        (0, 1): DMA_W(xtok.ap()[:, 4:8, :].rearrange("p t f -> p (t f)"),
                      xin[0, 0][:, 4096:8192]),
    }
    pref = {
        0: DMA_W(wbuf.ap()[:, 0], wqkvo[1].rearrange("p (c f) -> p c f", f=1024)),
        1: DMA_W(wbuf.ap()[:, 1], wqkvo[2].rearrange("p (c f) -> p c f", f=1024)),
    }
    mz16 = V(lambda v: v.memset(zt16.ap()[:], 0.0), waits=((sX, cst_all),))
    zi_mark = 0
    for k in range(NBLK):
        zi_mark = DMA_W(tokd[k * 128:(k + 1) * 128, :], zt16.ap()[:],
                        waits=((sV, mz16),) if k == 0 else ())
    mir_box = [0]

    def emit_mirror_build():
        units = []
        for e in range(E):
            units.append((w1[e][:, 0:4096], w1m[e * 128:(e + 1) * 128, 0:4096]))
            units.append((w1[e][:, 4096:8192],
                          w1m[e * 128:(e + 1) * 128, 4096:8192]))
            units.append((w2[e], w2m[e * 128:(e + 1) * 128, :]))
        ld_m = {}
        st_m = {}
        for u, (src_ap, dst_ap) in enumerate(units):
            c0 = (u % 2) * 4096
            ld_m[u] = DMA_W(zb_bf_stage[:, c0:c0 + 4096], src_ap,
                            waits=((sX, st_m.get(u - 2, 0)),))
            st_m[u] = DMA_X(dst_ap, zb_bf_stage[:, c0:c0 + 4096],
                            waits=((sW, ld_m[u]),))
        mir_box[0] = st_m[len(units) - 1]
    V(lambda v: v.memset(cbias.ap()[:], SM_BIAS))
    V(lambda v: v.memset(cscale.ap()[:], 0.125))
    V(lambda v: v.memset(ceps.ap()[:], EPS))

    ln_prev_act = {}

    def ln_stats_v(sl, src_ap, waits_v=()):
        wv = list(waits_v)
        if sl in ln_prev_act:
            wv.append((sS, ln_prev_act[sl]))
        V(lambda v: v.bn_stats(st12.ap()[:, 12 * sl:12 * sl + 6], src_ap[:, 0:512]),
          waits=wv)
        V(lambda v: v.bn_stats(st12.ap()[:, 12 * sl + 6:12 * sl + 12],
                               src_ap[:, 512:1024]), waits=waits_v)
        return V(lambda v: v.bn_aggr(mvs.ap()[:, 2 * sl:2 * sl + 2],
                                     st12.ap()[:, 12 * sl:12 * sl + 12]))

    def ln_rstd(sl, mstats):
        a0 = 8 * sl
        mL = S(lambda s: s.activation(aux.ap()[:, a0:a0 + 1],
                                      mvs.ap()[:, 2 * sl + 1:2 * sl + 2], AF.Ln,
                                      bias=ceps.ap()[:, 0:1]), waits=((sV, mstats),))
        ln_prev_act[sl] = mL
        mS2 = S(lambda s: s.activation(aux.ap()[:, a0 + 1:a0 + 2],
                                       aux.ap()[:, a0:a0 + 1], AF.Exp, scale=-0.5))
        V(lambda v: v.tensor_scalar(aux.ap()[:, a0 + 2:a0 + 3],
                                    mvs.ap()[:, 2 * sl:2 * sl + 1],
                                    aux.ap()[:, a0 + 1:a0 + 2], -1.0,
                                    ALU.mult, ALU.mult), waits=((sS, mS2),))
        return V(lambda v: v.tensor_scalar(aux.ap()[:, a0 + 3:a0 + 4],
                                           mvs.ap()[:, 2 * sl:2 * sl + 1], -1.0,
                                           None, ALU.mult))

    def gate_tile(qt, gate_first_waits, pre_war):
        def gmm(te, qt=qt):
            for dcc in range(DC):
                ins = te.matmul(
                    psA.ap()[:, qt % 2, 0:16],
                    lnfm.ap()[:, dcc, qt * 128:(qt + 1) * 128],
                    wgb.ap()[:, dcc, :],
                    start=(dcc == 0), stop=(dcc == DC - 1))
            return ins
        mt = T(gmm, waits=gate_first_waits)
        V(lambda v, qt=qt: v.reduce_max(aux.ap()[:, 16:17], psA.ap()[:, qt % 2, 0:16],
                                        axis=AX.X), waits=((sT, mt),))
        V(lambda v: v.tensor_scalar(aux.ap()[:, 17:18], aux.ap()[:, 16:17], -1.0,
                                    None, ALU.mult))
        mex = S(lambda s, qt=qt: s.activation(
            gex.ap()[:], psA.ap()[:, qt % 2, 0:16], AF.Exp, bias=aux.ap()[:, 17:18]),
            waits=((sV, sV.n),))
        V(lambda v: v.reduce_sum(aux.ap()[:, 18:19], gex.ap()[:], axis=AX.X),
          waits=((sS, mex),))
        V(lambda v: v.reciprocal(aux.ap()[:, 19:20], aux.ap()[:, 18:19]))
        V(lambda v: v.tensor_scalar(gsc.ap()[:], gex.ap()[:], aux.ap()[:, 19:20],
                                    None, ALU.mult))
        V(lambda v: v.tensor_scalar(gtmp.ap()[:], iota16b.ap()[:], -1e-6,
                                    None, ALU.mult))
        V(lambda v: v.tensor_add(gsc.ap()[:], gsc.ap()[:], gtmp.ap()[:]))
        V(lambda v: v.reduce_max(aux.ap()[:, 16:17], gsc.ap()[:], axis=AX.X))
        V(lambda v: v.tensor_scalar(gtmp.ap()[:], gsc.ap()[:], aux.ap()[:, 16:17],
                                    None, ALU.is_equal))
        V(lambda v: v.scalar_tensor_tensor(gtmp.ap()[:], gtmp.ap()[:], -1e30,
                                           gsc.ap()[:], ALU.mult, ALU.add))
        V(lambda v: v.reduce_max(aux.ap()[:, 16:17], gtmp.ap()[:], axis=AX.X))
        V(lambda v: v.tensor_scalar(gtmp.ap()[:], gsc.ap()[:], aux.ap()[:, 16:17],
                                    None, ALU.is_ge))
        mind = V(lambda v: v.tensor_copy(indbf.ap()[:], gtmp.ap()[:]))

        def pmm(te, qt=qt):
            return te.matmul(psA.ap()[:, qt % 2, 32:48], ultrib.ap()[:],
                             indbf.ap()[:], start=True, stop=True)
        mpre = T(pmm, waits=[(sV, mind), (sS, mex), (sX, cst_x), (sV, pre_war)])

        def cmm(te, qt=qt):
            return te.matmul(psA.ap()[0:1, qt % 2, 512:528], ones1bb.ap()[:],
                             indbf.ap()[:], start=True, stop=True)
        mcnt = T(cmm, waits=((sV, mind),))
        V(lambda v, qt=qt: v.tensor_copy(prefb.ap()[:, qt, :],
                                         psA.ap()[:, qt % 2, 32:48]),
          waits=((sT, mpre),))
        V(lambda v, qt=qt: v.tensor_copy(cnts_sb.ap()[0:1, 16 * qt:16 * qt + 16],
                                         psA.ap()[0:1, qt % 2, 512:528]),
          waits=((sT, mcnt),))
        V(lambda v: v.reduce_max(aux.ap()[:, 20:21], gsc.ap()[:], axis=AX.X))
        V(lambda v: v.tensor_scalar(gex.ap()[:], gsc.ap()[:], aux.ap()[:, 20:21],
                                    None, ALU.is_equal))
        V(lambda v: v.tensor_mul(s16a.ap()[:], iota16b.ap()[:], gex.ap()[:]))
        V(lambda v, qt=qt: v.reduce_sum(selp.ap()[:, qt, 0:1], s16a.ap()[:],
                                        axis=AX.X))
        V(lambda v: v.tensor_sub(gtmp.ap()[:], gtmp.ap()[:], gex.ap()[:]))
        V(lambda v: v.tensor_mul(s16a.ap()[:], iota16b.ap()[:], gtmp.ap()[:]))
        V(lambda v, qt=qt: v.reduce_sum(selp.ap()[:, qt, 1:2], s16a.ap()[:],
                                        axis=AX.X))
        V(lambda v, qt=qt: v.tensor_copy(cwvsb.ap()[:, qt, 0:1],
                                         aux.ap()[:, 20:21]))
        mrt = V(lambda v, qt=qt: v.tensor_copy(cwvsb.ap()[:, qt, 1:2],
                                               aux.ap()[:, 16:17]))
        return mex, mrt

    def ln_to_fm(xwaits, with_gate=False, gate_deps=None):
        tp_mark = {}
        cp_mark = {}
        mex = {}
        mrt = {}
        mstats = {}
        mnorm_mark = {}
        lnd_mark = {}

        def chain(tt):
            sl = tt % 2
            a0 = 8 * sl
            mst = ln_rstd(sl, mstats[tt])
            nwaits = [(sV, mst)]
            if tt >= 2:
                nwaits.append((sT, tp_mark[tt - 2]))
                if lnd_mark.get(tt - 2):
                    nwaits.append((sX, lnd_mark[tt - 2]))
            elif gate_deps is not None:
                nwaits.extend(gate_deps.get("lnw01", ()))
            mnorm = S(lambda s, tt=tt, a0=a0, sl=sl: s.activation(
                lntok.ap()[:, sl, :], xtok.ap()[:, tt, :], AF.Identity,
                bias=aux.ap()[:, a0 + 2:a0 + 3], scale=aux.ap()[:, a0 + 1:a0 + 2]),
                waits=nwaits)
            mnorm_mark[tt] = mnorm
            if with_gate:
                lw = [(sS, mnorm)]
                if tt == 0 and gate_deps is not None:
                    lw.extend(gate_deps.get("lnd0", ()))
                lnd_mark[tt] = DMA_X(ln3d[tt * 128:(tt + 1) * 128, :],
                                     lntok.ap()[:, tt % 2, :], waits=tuple(lw))
            twaits = [(sS, mnorm)]
            if tt >= 2:
                twaits.append((sV, cp_mark[tt - 2]))
            elif gate_deps is not None and gate_deps.get("psb_war"):
                twaits.append((sV, gate_deps["psb_war"]))
            if gate_deps is not None:
                twaits.extend(gate_deps.get("te0", ()))

            def tp(te, tt=tt, sl=sl):
                for dcc in range(DC):
                    ins = te.transpose(psB_bf[:, sl, dcc * 128:(dcc + 1) * 128],
                                       lntok.ap()[:, sl, dcc * 128:(dcc + 1) * 128],
                                       identb.ap()[:])
                return ins
            tp_mark[tt] = T(tp, waits=twaits, lp="bf16 transpose")
            cwaits = [(sT, tp_mark[tt])]
            if tt < 2 and gate_deps is not None:
                cwaits.extend(gate_deps.get("cp0", ()))
            cp_mark[tt] = V(lambda v, tt=tt, sl=sl: v.tensor_copy(
                lnfm.ap()[:, :, tt * 128:(tt + 1) * 128],
                psB_bf[:, sl, 0:1024].rearrange("p (c q) -> p c q", c=DC)),
                waits=tuple(cwaits))

        def gate_emit(qt):
            gw = [(sV, cp_mark[qt]), (sW, wg_ld)]
            if qt >= 2:
                gw.append((sS, mex[qt - 2]))
                gw.append((sV, mrt[qt - 2]))
            elif gate_deps is not None and gate_deps.get("psa_war"):
                gw.append((sS, gate_deps["psa_war"]))
            mex[qt], mrt[qt] = gate_tile(qt, gw, mrt.get(qt - 2, 0))

        for tt in range(TT):
            mstats[tt] = ln_stats_v(tt % 2, xtok.ap()[:, tt, :],
                                    waits_v=xwaits.get(tt, ()))
            if tt >= 1:
                chain(tt - 1)
                if with_gate and tt >= 2:
                    gate_emit(tt - 2)
        chain(TT - 1)
        if with_gate:
            gate_emit(TT - 2)
            gate_emit(TT - 1)
        return tp_mark, cp_mark, mex, mrt, mnorm_mark, lnd_mark

    car_lnw01 = ()
    car_cp0 = ()
    car_ln3d = ()
    car_tok = ()
    car_tid = ()
    car_idx = ()
    car_pd2 = 0
    car_sil = 0
    car_tt = 0
    car_sg = {0: 0, 1: 0}
    car_w2c15 = 0

    for p in range(NPASS):
        b, w = p // 2, p % 2

        xw = {0: ((sW, xld[(p, 0)]),), 4: ((sW, xld[(p, 1)]),)}
        tp1, cp1, _, _, _, _ = ln_to_fm(
            xw, gate_deps={"lnw01": car_lnw01, "cp0": car_cp0,
                           "te0": ((sX, cst_all),) if p == 0 else ()})

        rot = [psA.ap()[:, 0], psA.ap()[:, 1], psB.ap()[:, 0], psB.ap()[:, 1]]
        nrot = [0]
        evac_mark = {}

        def fm_proj_group(oc, lhs_half, extra_waits):
            slot = nrot[0] % 4
            nrot[0] += 1
            waits = list(extra_waits)
            if slot in evac_mark:
                waits.append(evac_mark[slot])

            def mms(te, oc=oc, slot=slot, lhs_half=lhs_half):
                for dcc in range(DC):
                    for h2 in range(2):
                        ins = te.matmul(
                            rot[slot][:, h2 * 512:(h2 + 1) * 512],
                            wbuf.ap()[:, lhs_half, dcc, oc * 128:(oc + 1) * 128],
                            lnfm.ap()[:, dcc, h2 * 512:(h2 + 1) * 512],
                            start=(dcc == 0), stop=(dcc == DC - 1))
                return ins
            return slot, T(mms, waits=waits)

        lnfm_all = (sV, cp1[TT - 1])
        if w == 0:
            for oc in range(DC):
                ew = [(sW, pref[0]), lnfm_all] if oc == 0 else [(sW, pref[0])]
                slot, mt = fm_proj_group(oc, 0, ew)
                kw = [(sT, mt)]
                evac_mark[slot] = (sV, V(lambda v, oc=oc, slot=slot: v.tensor_copy(
                    kfm.ap()[:, oc, :], rot[slot]), waits=kw,
                    lp="k evac bf16"))
            k_done = sT.n
            mq = DMA_W(wbuf.ap()[:, 0], wqkvo[0].rearrange("p (c f) -> p c f", f=1024),
                       waits=((sT, k_done),))
            V(lambda v: v.memset(
                vaug.ap()[:].rearrange("p t (h c) -> p t h c", c=65)[:, :, :, 64:65], 1.0))
            for kt in range(TT):
                slot = nrot[0] % 4
                nrot[0] += 1
                waits = [(sW, pref[1])]
                if slot in evac_mark:
                    waits.append(evac_mark[slot])

                def mms(te, kt=kt, slot=slot):
                    for dcc in range(DC):
                        for h2 in range(2):
                            ins = te.matmul(
                                rot[slot][:, h2 * 512:(h2 + 1) * 512],
                                lnfm.ap()[:, dcc, kt * 128:(kt + 1) * 128],
                                wbuf.ap()[:, 1, dcc, h2 * 512:(h2 + 1) * 512],
                                start=(dcc == 0), stop=(dcc == DC - 1))
                    return ins
                mt = T(mms, waits=waits)
                evac_mark[slot] = (sV, V(lambda v, kt=kt, slot=slot: v.tensor_copy(
                    vaug.ap()[:, kt, :].rearrange("p (h c) -> p h c", c=65)[:, :, 0:64],
                    rot[slot].rearrange("p (h d) -> p h d", d=64)),
                    waits=((sT, mt),), lp="v to bf16"))
            v_done = sT.n
            mo = DMA_W(wbuf.ap()[:, 1], wqkvo[3].rearrange("p (c f) -> p c f", f=1024),
                       waits=((sT, v_done),))
            for oc in range(DC):
                slot, mt = fm_proj_group(oc, 0, [(sW, mq)])
                evac_mark[slot] = (sV, V(lambda v, oc=oc, slot=slot: v.tensor_copy(
                    qfm[:, oc, :], rot[slot]), waits=((sT, mt),),
                    lp="q evac bf16"))
            vaug_ready = sV.n
        else:
            mo = pref[1]
            for oc in range(DC):
                ew = [(sW, pref[0]), lnfm_all] if oc == 0 else [(sW, pref[0])]
                slot, mt = fm_proj_group(oc, 0, ew)
                evac_mark[slot] = (sV, V(lambda v, oc=oc, slot=slot: v.tensor_copy(
                    qfm[:, oc, :], rot[slot]), waits=((sT, mt),),
                    lp="q evac bf16"))
            vaug_ready = 0
        q_ev_last = sV.n
        p2_done = sT.n
        if p == 0:
            emit_mirror_build()
        mir_done = mir_box[0]

        e_prev = None
        a_prev_mark = None
        mRc_prev = 0
        monorm_prev = 0
        for h in range(NH):
            hc, hr = h // 2, (h % 2) * 64
            oslot = h % 2
            e_mark = {}
            a_mark = {}
            m_marks = {}

            def emit_av(kt, h=h, oslot=oslot):
                q0 = 128 * kt
                avw = [(sV, m_marks[kt])]
                if kt == 0:
                    if h == 0 and w == 0:
                        avw.append((sV, vaug_ready))
                    if h > 0:
                        avw.append((sV, mRc_prev))

                def avmm(te, kt=kt, q0=q0, h=h, oslot=oslot):
                    for c0 in (0, 512):
                        lo, hi = max(q0, c0), c0 + 512
                        if lo >= hi:
                            continue
                        ins = te.matmul(
                            psB.ap()[0:65, oslot, lo:hi],
                            vaug.ap()[:, kt, h * 65:h * 65 + 65],
                            Eb[:, kt % 2, lo - q0:hi - q0],
                            start=(kt == 0),
                            stop=(kt == 3 if c0 == 0 else kt == TT - 1))
                    return ins
                a_mark[kt] = T(avmm, waits=avw)

            for kt in range(TT):
                q0 = 128 * kt
                swaits = []
                if h == 0 and kt == 0:
                    swaits.append((sV, q_ev_last))
                if kt >= 2:
                    swaits.append((sS, e_mark[kt - 2]))
                elif e_prev is not None:
                    swaits.append((sS, e_prev[(6, 7)[kt]]))

                def smm(te, kt=kt, q0=q0, hc=hc, hr=hr):
                    for c0 in (0, 512):
                        lo, hi = max(c0, q0), c0 + 512
                        if lo >= hi:
                            continue
                        ins = te.matmul(
                            psA.ap()[:, kt % 2, lo:hi],
                            kfm.ap()[hr:hr + 64, hc, q0:q0 + 128],
                            qfm[hr:hr + 64, hc, lo:hi],
                            start=True, stop=True)
                    return ins
                s_mark = T(smm, waits=swaits)
                ewaits = [(sT, s_mark)]
                if kt >= 2:
                    ewaits.append((sT, a_mark[kt - 2]))
                elif a_prev_mark is not None:
                    ewaits.append((sT, a_prev_mark[(6, 7)[kt]]))
                e_mark[kt] = S(lambda s, kt=kt, q0=q0: s.activation(
                    Eb[:, kt % 2, 0:1024 - q0], psA.ap()[:, kt % 2, q0:1024],
                    AF.Exp, bias=cbias.ap()[:, 0:1], scale=cscale.ap()[:, 0:1]),
                    waits=ewaits)
                m_marks[kt] = V(lambda v, kt=kt: v.tensor_tensor(
                    Eb[:, kt % 2, 0:128], Eb[:, kt % 2, 0:128], diagb.ap()[:], ALU.mult),
                    waits=((sS, e_mark[kt]),))
                if kt >= 1:
                    emit_av(kt - 1)
            emit_av(TT - 1)
            monorm = 0
            for c0, amk in ((0, a_mark[3]), (512, a_mark[TT - 1])):
                S(lambda s, oslot=oslot, c0=c0: s.activation(
                    t32.ap()[0:1, c0:c0 + 512],
                    psB.ap()[64:65, oslot, c0:c0 + 512], AF.Ln),
                    waits=((sT, amk),))
                mrec = S(lambda s, c0=c0: s.activation(
                    rb[:, c0:c0 + 512], t32.ap()[0:1, c0:c0 + 512], AF.Exp,
                    scale=-1.0))

                def rmm(te, oslot=oslot, c0=c0):
                    return te.matmul(psB.ap()[0:64, 1 - oslot, c0:c0 + 512],
                                     ones64b.ap()[:], rb[:, c0:c0 + 512],
                                     start=True, stop=True)
                mr = T(rmm, waits=[(sS, mrec)]
                       + ([(sV, monorm_prev)] if h > 0 and c0 == 0 else []))
                mRc = V(lambda v, oslot=oslot, c0=c0: v.tensor_copy(
                    Rb[:, c0:c0 + 512], psB.ap()[0:64, 1 - oslot, c0:c0 + 512]),
                    waits=((sT, mr),), lp="R to bf16")
                monorm = V(lambda v, hc=hc, hr=hr, oslot=oslot, c0=c0:
                           v.tensor_tensor(
                               ofm[hr:hr + 64, hc, c0:c0 + 512],
                               psB.ap()[0:64, oslot, c0:c0 + 512],
                               Rb[:, c0:c0 + 512], ALU.mult),
                           waits=((sV, mRc),), lp="attn out to bf16")
            a_prev_mark = a_mark
            e_prev = e_mark
            mRc_prev = mRc
            monorm_prev = monorm
        p3_done = sT.n
        monorm_last = monorm_prev

        ev2 = {}
        p4_stats = {}
        mact4_box = [0]

        def p4_chain(qt):
            sl = qt % 2
            a0 = 8 * sl
            mr4 = ln_rstd(sl, p4_stats[qt])
            mact4_box[0] = S(lambda s, sl=sl, a0=a0: s.activation(
                t32.ap()[:], psA.ap()[:, sl], AF.Identity,
                bias=aux.ap()[:, a0 + 3:a0 + 4]),
                waits=((sV, mr4), (sV, ev2.get(1 - sl, 0))))
            ev2[sl] = V(lambda v, qt=qt, a0=a0: v.scalar_tensor_tensor(
                xtok.ap()[:, qt, :], t32.ap()[:], aux.ap()[:, a0 + 1:a0 + 2],
                xtok.ap()[:, qt, :], ALU.mult, ALU.add),
                waits=((sS, mact4_box[0]),))

        for qt in range(TT):
            slot = qt % 2
            waits = [(sW, mo)]
            if qt == 0:
                waits.append((sV, monorm_last))
            if slot in ev2:
                waits.append((sV, ev2[slot]))

            def womm(te, qt=qt, slot=slot):
                for dcc in range(DC):
                    for h2 in range(2):
                        ins = te.matmul(
                            psA.ap()[:, slot, h2 * 512:(h2 + 1) * 512],
                            ofm[:, dcc, qt * 128:(qt + 1) * 128],
                            wbuf.ap()[:, 1, dcc, h2 * 512:(h2 + 1) * 512],
                            start=(dcc == 0), stop=(dcc == DC - 1))
                return ins
            mt = T(womm, waits=waits)
            p4_stats[qt] = ln_stats_v(slot, psA.ap()[:, slot], waits_v=((sT, mt),))
            if qt >= 1:
                p4_chain(qt - 1)
        p4_chain(TT - 1)
        mact4 = mact4_box[0]
        wo_done = sT.n

        mws1a = DMA_W(wbuf.ap()[:, 0],
                      ws1.rearrange("p (c s) -> p c s", s=2048)[:, :, 0:1024],
                      waits=((sT, p2_done),))
        mws1b = DMA_W(wbuf.ap()[:, 1],
                      ws1.rearrange("p (c s) -> p c s", s=2048)[:, :, 1024:2048],
                      waits=((sT, wo_done),))

        car_lnw01_p5 = ((sT, tp1[TT - 1]),)
        mrun0 = V(lambda v: v.memset(run_sb.ap()[:], 0.0))
        tp5, cp5, mex, mrt5, mnorm5, lnd5 = ln_to_fm(
            {}, with_gate=True,
            gate_deps={"psb_war": monorm_last, "psa_war": mact4,
                       "lnw01": car_lnw01_p5, "lnd0": car_ln3d})

        mws2 = DMA_W(ws2b[:, :, :], ws2.rearrange("p (c f) -> p c f", f=1024),
                     waits=((sT, p3_done),))
        sg_cons = {0: car_sg[0], 1: car_sg[1]}
        last_sil = car_sil
        last_tt = car_tt
        ys_g_done = 0
        ys_u_done = 0
        for i in range(TT):
            def gmms6(te, i=i):
                off = i * 128
                for dcc in range(DC):
                    for h2 in range(2):
                        ins = te.matmul(
                            psA.ap()[:, 0, h2 * 512:(h2 + 1) * 512],
                            wbuf.ap()[:, 1, dcc, off:off + 128],
                            lnfm.ap()[:, dcc, h2 * 512:(h2 + 1) * 512],
                            start=(dcc == 0), stop=(dcc == DC - 1))
                return ins
            gw6 = [(sW, mws1b), (sS, last_sil)]
            if i == 0:
                gw6.append((sS, mex[6]))
                gw6.append((sV, mrt5[6]))
            ys_g_done = T(gmms6, waits=gw6)
            silw6 = [(sT, ys_g_done), (sV, sg_cons.get(i % 2, 0))]
            if i < 2:
                silw6.append((sX, lnd5[(6, 7)[i]]))
            last_sil = S(lambda s, i=i: s.activation(
                sgt[:, i % 2, :], psA.ap()[:, 0], AF.Silu), waits=silw6)

            def umms6(te, i=i):
                off = i * 128
                for dcc in range(DC):
                    for h2 in range(2):
                        ins = te.matmul(
                            psA.ap()[:, 1, h2 * 512:(h2 + 1) * 512],
                            wbuf.ap()[:, 0, dcc, off:off + 128],
                            lnfm.ap()[:, dcc, h2 * 512:(h2 + 1) * 512],
                            start=(dcc == 0), stop=(dcc == DC - 1))
                return ins
            uw6 = [(sW, mws1a), (sV, last_tt)]
            if i == 0:
                uw6.append((sS, mex[7]))
                uw6.append((sV, mrt5[7]))
            ys_u_done = T(umms6, waits=uw6)
            last_tt = V(lambda v, i=i: v.tensor_tensor(
                ysp[:, i, :], psA.ap()[:, 1], sgt[:, i % 2, :], ALU.mult),
                waits=((sT, ys_u_done), (sS, last_sil)), lp="ys_pre to bf16")
            sg_cons[i % 2] = last_tt
        ysp_done = last_tt

        for qt in range(TT):
            V(lambda v, qt=qt: v.tensor_copy(
                tbase_sb.ap()[0:1, 16 * qt:16 * qt + 16], run_sb.ap()[:]),
              waits=((sV, mrt5[qt]), (sV, mrun0)) if qt == 0 else ((sV, mrt5[qt]),))
            V(lambda v, qt=qt: v.tensor_add(
                run_sb.ap()[:], run_sb.ap()[:],
                cnts_sb.ap()[0:1, 16 * qt:16 * qt + 16]))
        V(lambda v: v.tensor_scalar(r128g.ap()[:], run_sb.ap()[:], 127.0,
                                    1.0 / 128.0, ALU.add, ALU.mult))
        V(lambda v: v.tensor_copy(r128i.ap()[:], r128g.ap()[:]))
        V(lambda v: v.tensor_copy(r128f.ap()[:], r128i.ap()[:]))
        V(lambda v: v.tensor_tensor(r128h.ap()[:], r128f.ap()[:], r128g.ap()[:],
                                    ALU.is_gt))
        V(lambda v: v.tensor_sub(r128f.ap()[:], r128f.ap()[:], r128h.ap()[:]))
        V(lambda v: v.tensor_scalar(r128f.ap()[:], r128f.ap()[:], 128.0, None,
                                    ALU.mult))
        V(lambda v: v.memset(rbase_sb.ap()[0:1, 0:1], 0.0))
        for e2_ in range(1, 16):
            V(lambda v, e2_=e2_: v.tensor_add(
                rbase_sb.ap()[0:1, e2_:e2_ + 1], rbase_sb.ap()[0:1, e2_ - 1:e2_],
                r128f.ap()[0:1, e2_ - 1:e2_]))
        mrb = sV.n
        def rbmm(te):
            return te.matmul(psB.ap()[:, 0, 0:16], onesf1b.ap()[:],
                             rbase_sb.ap()[:], start=True, stop=True)
        mrbT = T(rbmm, waits=((sV, mrb), (sV, cp5[TT - 1])))
        mrbV = V(lambda v: v.tensor_copy(rbaseb.ap()[:], psB.ap()[:, 0, 0:16]),
                 waits=((sT, mrbT),))
        V(lambda v: v.tensor_scalar(s16a.ap()[:], rbaseb.ap()[:],
                                    pos128b.ap()[:, 0:1], None, ALU.is_le))
        V(lambda v: v.reduce_sum(sl2f.ap()[:, 0:1], s16a.ap()[:], axis=AX.X))
        V(lambda v: v.tensor_scalar(sl2f.ap()[:, 0:1], sl2f.ap()[:, 0:1], -1.0,
                                    None, ALU.add))
        V(lambda v: v.memset(indbf.ap()[:], 0.0))
        mcpd = V(lambda v: v.tensor_copy(indbf.ap()[:, 0:1], sl2f.ap()[:, 0:1]))

        def temm(te):
            return te.transpose(psB_bf[0:16, 0, 1024:1152], indbf.ap()[:],
                                identb.ap()[:])
        mtrT = T(temm, waits=((sV, mcpd),), lp="bf16 transpose")
        mblkrow = V(lambda v: v.tensor_copy(blkrow_sb.ap()[:],
                                            psB_bf[0:1, 0, 1024:1056]),
                    waits=((sT, mtrT),))
        for blk in range(NBLK):
            def bmm(te, blk=blk):
                return te.matmul(psB.ap()[:, 1, blk:blk + 1], onesf1b.ap()[:],
                                 blkrow_sb.ap()[0:1, blk:blk + 1],
                                 start=True, stop=True)
            T(bmm, waits=((sV, mblkrow),) if blk == 0 else ())
        midxT = sT.n
        V(lambda v: v.scalar_tensor_tensor(
            idxwf.ap()[:], psB.ap()[:, 1, 0:NBLK], 128.0, iota128b.ap()[:],
            ALU.mult, ALU.add), waits=((sT, midxT),) + car_idx)
        midxw = V(lambda v: v.tensor_copy(idxwi.ap()[:], idxwf.ap()[:]))
        scat_last = 0
        slot_mark = {}
        tid_mark = 0
        for qt in range(TT):
            def tbmm(te, qt=qt):
                return te.matmul(psB.ap()[:, 0, 16:32], onesf1b.ap()[:],
                                 tbase_sb.ap()[0:1, 16 * qt:16 * qt + 16],
                                 start=True, stop=True)
            mtb = T(tbmm, waits=((sV, slot_mark[qt - 1]),) if qt else
                    ((sV, mrbV),))
            V(lambda v, qt=qt: v.tensor_tensor(slotf.ap()[:],
                                               psB.ap()[:, 0, 16:32],
                                               rbaseb.ap()[:], ALU.add),
              waits=((sT, mtb),))
            V(lambda v, qt=qt: v.tensor_add(slotf.ap()[:], slotf.ap()[:],
                                            prefb.ap()[:, qt, :]))
            for k in (0, 1):
                V(lambda v, qt=qt, k=k: v.tensor_scalar(
                    s16a.ap()[:], iota16b.ap()[:], selp.ap()[:, qt, k:k + 1],
                    None, ALU.is_equal))
                V(lambda v: v.tensor_mul(s16a.ap()[:], slotf.ap()[:], s16a.ap()[:]))
                V(lambda v, k=k: v.reduce_sum(sl2f.ap()[:, k:k + 1], s16a.ap()[:],
                                              axis=AX.X))
            V(lambda v: v.tensor_scalar(sl2f.ap()[:], sl2f.ap()[:],
                                        float(NSLOT - 1), None, ALU.min))
            swts = [(sW, car_pd2)] if (qt == 0 and car_pd2) else []
            slot_mark[qt] = V(lambda v, qt=qt: v.tensor_copy(
                slot2i.ap()[:, qt, :], sl2f.ap()[:]), waits=swts)
            tid_mark = V(lambda v, qt=qt: v.tensor_scalar(
                tidv.ap()[:], tokp16b.ap()[:], float(qt * 128), None, ALU.add),
                waits=[(sW, scat_last)] + (list(car_tid) if qt == 0 else []))
            for k in (0, 1):
                sww = [(sV, slot_mark[qt]), (sV, tid_mark)]
                if qt == 0 and k == 0:
                    sww.extend(car_tok)
                scat_last = DMA_SCAT(tokd[:, :], slot2i.ap()[:, qt, k:k + 1],
                                     tidv.ap()[:], waits=sww)

        w1ldb = {}
        w2ldb = {}
        tibld = {}
        tcast = {}
        xgld = {}
        xgtr = {}
        xfmcp = {}
        gu_done = {}
        sil_m = {}
        hpTT = {}
        hptr = {}
        hpcp = {}
        w2mm = {}
        pts = {}
        pw = {}

        def emit_blk_loads(blk):
            w1w = [(sV, midxw)]
            if p == 0 and blk < 2:
                w1w.append((sX, mir_done))
            if blk < 2:
                w1w.append((sT, ys_u_done))
            else:
                w1w.append((sT, gu_done[blk - 2]))
            w1ldb[blk] = DMA_G(
                wbuf.ap()[:, blk % 2].rearrange("p c f -> p (c f)"),
                w1m[:, :], idxwi.ap()[:, blk:blk + 1], waits=w1w)
            tbw = [(sW, scat_last)]
            if blk >= 2:
                tbw.append((sV, tcast[blk - 2]))
            tibld[blk] = DMA_X(tib.ap()[:, blk % 2, :],
                               tokd[blk * 128:(blk + 1) * 128, :], waits=tbw)
            tcw = [(sX, tibld[blk])]
            if blk >= 2:
                tcw.append((sW, xgld[blk - 2]))
            tcast[blk] = V(lambda v, blk=blk: v.tensor_copy(
                tii.ap()[:, blk % 2, :], tib.ap()[:, blk % 2, 0:1]), waits=tcw)
            xgw = [(sV, tcast[blk])]
            if blk < 2:
                xgw.append((sX, lnd5[TT - 1]))
            else:
                xgw.append((sT, xgtr[blk - 2]))
            xgld[blk] = DMA_G(xg2[:, blk % 2, :], ln3d[:, :],
                              tii.ap()[:, blk % 2, 0:1], waits=xgw)

        emit_blk_loads(0)
        emit_blk_loads(1)
        w2ldb[0] = DMA_G(
            w2buf.ap()[:].rearrange("p c f -> p (c f)"), w2m[:, :],
            idxwi.ap()[:, 0:1],
            waits=[(sV, midxw)] + ([(sT, car_w2c15)] if car_w2c15 else []))
        for blk in range(NBLK):
            def xtr(te, blk=blk):
                for dcc in range(DC):
                    ins = te.transpose(
                        psA_bf[:, 1, 1024 + dcc * 128:1024 + (dcc + 1) * 128],
                        xg2[:, blk % 2, dcc * 128:(dcc + 1) * 128],
                        identb.ap()[:])
                return ins
            xw7 = [(sW, xgld[blk]), (sS, xfmcp.get(blk - 1, 0))]
            if blk < 2:
                xw7.append((sV, ysp_done))
            xgtr[blk] = T(xtr, waits=xw7, lp="bf16 transpose")
            fcw = [(sT, xgtr[blk])]
            if blk < 2:
                fcw.append((sT, ys_u_done))
            else:
                fcw.append((sT, gu_done[blk - 2]))
            xfmcp[blk] = S(lambda s, blk=blk: s.activation(
                xfm[:, blk % 2].rearrange("p c f -> p (c f)"),
                psA_bf[:, 1, 1024:2048], AF.Identity), waits=fcw)

            def gmm7(te, blk=blk):
                for dcc in range(DC):
                    ins = te.matmul(psA.ap()[:, 0, 512:1024],
                                    xfm[:, blk % 2, dcc, :],
                                    wbuf.ap()[:, blk % 2, dcc, 512:1024],
                                    start=(dcc == 0), stop=(dcc == DC - 1))
                return ins
            gw7 = [(sW, w1ldb[blk]), (sS, xfmcp[blk]),
                   (sS, sil_m.get(blk - 1, 0))]
            g_done = T(gmm7, waits=gw7)
            sil_m[blk] = S(lambda s, blk=blk: s.activation(
                sgt[:, blk % 2, 0:512], psA.ap()[:, 0, 512:1024], AF.Silu),
                waits=((sT, g_done), (sV, sg_cons[blk % 2])))

            def umm7(te, blk=blk):
                for dcc in range(DC):
                    ins = te.matmul(psA.ap()[:, 0, 0:512],
                                    xfm[:, blk % 2, dcc, :],
                                    wbuf.ap()[:, blk % 2, dcc, 0:512],
                                    start=(dcc == 0), stop=(dcc == DC - 1))
                return ins
            gu_done[blk] = T(umm7, waits=((sV, hpTT.get(blk - 1, 0)),))
            hpTT[blk] = V(lambda v, blk=blk: v.tensor_tensor(
                hpf[:, :], psA.ap()[:, 0, 0:512], sgt[:, blk % 2, 0:512],
                ALU.mult), waits=((sT, gu_done[blk]), (sS, sil_m[blk]),
                                  (sT, hptr.get(blk - 1, 0))), lp="h to bf16")
            sg_cons[blk % 2] = hpTT[blk]

            def htr(te, blk=blk):
                for hcc in range(4):
                    ins = te.transpose(
                        psA_bf[:, 1, hcc * 128:(hcc + 1) * 128],
                        hpf[:, hcc * 128:(hcc + 1) * 128], identb.ap()[:])
                return ins
            hptr[blk] = T(htr, waits=((sV, hpTT[blk]),
                                      (sS, hpcp.get(blk - 1, 0))),
                          lp="bf16 transpose")
            hpcp[blk] = S(lambda s, blk=blk: s.activation(
                hpTs[:, :], psA_bf[:, 1, 0:512], AF.Identity),
                waits=((sT, hptr[blk]), (sT, w2mm.get(blk - 1, 0))))

            def w2mms(te, blk=blk):
                for hcc in range(4):
                    for h2 in range(2):
                        ins = te.matmul(
                            psB.ap()[:, blk % 2, h2 * 512:(h2 + 1) * 512],
                            hpTs[:, hcc * 128:(hcc + 1) * 128],
                            w2buf.ap()[:, hcc, h2 * 512:(h2 + 1) * 512],
                            start=(hcc == 0), stop=(hcc == 3))
                return ins
            w2w7 = [(sW, w2ldb[blk]), (sS, hpcp[blk]),
                    (sV, pts.get(blk - 2, 0))]
            if blk < 2:
                w2w7.append((sV, slot_mark[TT - 1]))
            w2mm[blk] = T(w2mms, waits=w2w7)
            pts[blk] = V(lambda v, blk=blk: v.tensor_copy(
                psc[:, blk % 2, :], psB.ap()[:, blk % 2]),
                waits=((sT, w2mm[blk]), (sE, pw.get(blk - 2, 0))),
                lp="expert out staged bf16")
            pww = [(sV, pts[blk])]
            if blk == 0 and car_pd2:
                pww.append((sW, car_pd2))
            pw[blk] = DMA_SE(pdram[blk * 128:(blk + 1) * 128, :],
                             psc[:, blk % 2, :], waits=pww)
            if blk + 1 < NBLK:
                w2ldb[blk + 1] = DMA_G(
                    w2buf.ap()[:].rearrange("p c f -> p (c f)"), w2m[:, :],
                    idxwi.ap()[:, blk + 1:blk + 2],
                    waits=((sV, midxw), (sT, w2mm[blk])))
            if blk + 2 < NBLK:
                emit_blk_loads(blk + 2)
        pw_final = pw[NBLK - 1]
        car_ln3d = ((sW, xgld[NBLK - 1]),)
        car_tok = ((sX, tibld[NBLK - 1]),)
        car_idx = ((sW, w2ldb[NBLK - 1]),)
        car_w2c15 = w2mm[NBLK - 1]
        car_sil = hpcp[NBLK - 1]
        car_tt = hpTT[NBLK - 1]
        car_sg = {0: hpTT[NBLK - 2], 1: hpTT[NBLK - 1]}
        car_cp0 = ((sE, pw_final),)

        xst = {}
        out_marks = {}
        p8_stats = {}
        mact8_box = [0]

        def p8_chain(qt):
            sl = qt % 2
            a0 = 8 * sl
            mr8 = ln_rstd(sl, p8_stats[qt])
            mact8_box[0] = S(lambda s, qt=qt, a0=a0: s.activation(
                t32.ap()[:], zb.ap()[:, qt, :], AF.Identity,
                bias=aux.ap()[:, a0 + 3:a0 + 4]),
                waits=((sV, mr8), (sV, xst.get(qt - 1, 0))))
            xst[qt] = V(lambda v, qt=qt, a0=a0: v.scalar_tensor_tensor(
                xtok.ap()[:, qt, :], t32.ap()[:], aux.ap()[:, a0 + 1:a0 + 2],
                xtok.ap()[:, qt, :], ALU.mult, ALU.add),
                waits=((sS, mact8_box[0]),))
            out_marks[qt] = DMA_X(yout[b, w][:, qt * 1024:(qt + 1) * 1024],
                                  xtok.ap()[:, qt, :], waits=((sV, xst[qt]),))

        gth = {}
        stt = {}
        for qt in range(TT):
            slot = qt % 2
            wswaits = [(sW, mws2), (sV, ysp_done)]
            if qt >= 2:
                wswaits.append((sV, xst[qt - 2]))
            else:
                wswaits.append((sV, pts[NBLK - 2 + qt]))

            def ws2mms(te, qt=qt, slot=slot):
                for hcc in range(DC):
                    for h2 in range(2):
                        ins = te.matmul(
                            psB.ap()[:, slot, h2 * 512:(h2 + 1) * 512],
                            ysp[:, hcc, qt * 128:(qt + 1) * 128],
                            ws2b[:, hcc, h2 * 512:(h2 + 1) * 512],
                            start=(hcc == 0), stop=(hcc == DC - 1))
                return ins
            mt = T(ws2mms, waits=wswaits)
            if qt >= 1:
                p8_chain(qt - 1)
            ziw = [(sT, mt)]
            if p == 0 and qt < 2:
                ziw.append((sX, mir_done))
            zini = V(lambda v, qt=qt, slot=slot: v.tensor_copy(
                zb.ap()[:, qt, :], psB.ap()[:, slot]), waits=ziw)
            for k in (0, 1):
                gbk = gb0 if k == 0 else gb1
                gw8 = [(sV, stt.get((qt - 1, k), 0))]
                if qt == 0 and k == 0:
                    gw8.append((sE, pw_final))
                gth[(qt, k)] = DMA_G(gbk, pdram[:, :],
                                     slot2i.ap()[:, qt, k:k + 1], waits=gw8)
                stt[(qt, k)] = V(
                    lambda v, qt=qt, gbk=gbk, k=k: v.scalar_tensor_tensor(
                        zb.ap()[:, qt, :], gbk, cwvsb.ap()[:, qt, k:k + 1],
                        zb.ap()[:, qt, :], ALU.mult, ALU.add),
                    waits=((sW, gth[(qt, k)]),))
            p8_stats[qt] = ln_stats_v(slot, zb.ap()[:, qt, :],
                                      waits_v=((sV, stt[(qt, 1)]),))
        p8_chain(TT - 1)
        car_pd2 = gth[(TT - 1, 1)]
        p8_u_done = sT.n

        if p + 1 < NPASS:
            nw = (p + 1) % 2
            idx0, idx1 = (1, 2) if nw == 0 else (0, 3)
            pref = {
                0: DMA_W(wbuf.ap()[:, 0],
                         wqkvo[idx0].rearrange("p (c f) -> p c f", f=1024),
                         waits=((sT, p8_u_done),)),
                1: DMA_W(wbuf.ap()[:, 1],
                         wqkvo[idx1].rearrange("p (c f) -> p c f", f=1024),
                         waits=((sT, p8_u_done),)),
            }
            xld[(p + 1, 0)] = DMA_W(
                xtok.ap()[:, 0:4, :].rearrange("p t f -> p (t f)"),
                xin[(p + 1) // 2, nw][:, 0:4096],
                waits=((sX, out_marks[3]),))
            xld[(p + 1, 1)] = DMA_W(
                xtok.ap()[:, 4:8, :].rearrange("p t f -> p (t f)"),
                xin[(p + 1) // 2, nw][:, 4096:8192],
                waits=((sX, out_marks[7]),))
        car_lnw01 = ((sX, lnd5[TT - 1]),)
        car_tid = ((sW, scat_last),)
        barrier()

    fin = sX.n
    on("sy", lambda eng: eng.wait_ge(sX.h, fin))

    with nc.Block() as block:
        @block.tensor
        def _(te):
            for f in events["te"]:
                f(te)

        @block.scalar
        def _(se):
            for f in events["se"]:
                f(se)

        @block.vector
        def _(ve):
            for f in events["ve"]:
                f(ve)

        @block.gpsimd
        def _(po):
            for f in events["po"]:
                f(po)

        @block.sync
        def _(sy):
            for f in events["sy"]:
                f(sy)

    return nc


_NC_CACHE = [None]


def kernel(L0_attn_hidden_states, L0_attn_Wq, L0_attn_Wk, L0_attn_Wv, L0_attn_Wo,
           L0_ffn_Wg, L0_ffn_W1, L0_ffn_W2, L0_ffn_Ws1, L0_ffn_Ws2):
    x = np.asarray(L0_attn_hidden_states)
    B, S, H = x.shape

    def pack_w(wt):
        d, f = wt.shape
        return np.ascontiguousarray(
            wt.reshape(d // 128, 128, f).transpose(1, 0, 2).reshape(128, (d // 128) * f))

    wqkvop = np.stack([pack_w(np.asarray(w).T.copy())
                       for w in (L0_attn_Wq, L0_attn_Wk, L0_attn_Wv, L0_attn_Wo)])
    wgp = pack_w(np.asarray(L0_ffn_Wg).T.copy())
    w1p = np.stack([pack_w(np.asarray(L0_ffn_W1)[e]) for e in range(E)])
    w2p = np.stack([pack_w(np.asarray(L0_ffn_W2)[e]) for e in range(E)])
    ws1p = pack_w(np.asarray(L0_ffn_Ws1))
    ws2p = pack_w(np.asarray(L0_ffn_Ws2))

    identc = np.eye(128, dtype=ml_dtypes.bfloat16)
    diagc = np.triu(np.ones((128, 128))).astype(ml_dtypes.bfloat16)
    ones64c = np.ones((1, 64), dtype=ml_dtypes.bfloat16)
    ultric = np.triu(np.ones((128, 128)), k=1).astype(ml_dtypes.bfloat16)
    iota16c = np.ascontiguousarray(
        np.broadcast_to(np.arange(16, dtype=np.float32), (128, 16)))
    iota128c = np.ascontiguousarray(np.broadcast_to(
        np.arange(128, dtype=np.float32).reshape(128, 1), (128, 32)))
    pos128c = (np.arange(128, dtype=np.float32) * 128.0).reshape(128, 1)
    onesf1c = np.ones((1, 128), dtype=np.float32)
    ones1bc = np.ones((128, 1), dtype=ml_dtypes.bfloat16)
    tokp16c = np.ascontiguousarray(np.broadcast_to(
        np.arange(128, dtype=np.float32).reshape(128, 1), (128, 16)))

    if _NC_CACHE[0] is None:
        _NC_CACHE[0] = build()
    nc = _NC_CACHE[0]

    in_maps = []
    for c in range(NCORES):
        xs = np.asarray(x[c * NB:(c + 1) * NB], dtype=np.float32)
        xp = np.ascontiguousarray(
            xs.reshape(NB, 2, TT, 128, 1024).transpose(0, 1, 3, 2, 4)
              .reshape(NB, 2, 128, TT * 1024))
        in_maps.append({
            "xp": xp, "wqkvop": wqkvop, "wgp": wgp, "w1p": w1p, "w2p": w2p,
            "ws1p": ws1p, "ws2p": ws2p, "identc": identc, "diagc": diagc,
            "ones64c": ones64c, "ultric": ultric, "iota16c": iota16c,
            "iota128c": iota128c, "pos128c": pos128c, "onesf1c": onesf1c,
            "ones1bc": ones1bc, "tokp16c": tokp16c,
        })

    res = run_bass_kernel_spmd(nc, in_maps, core_ids=list(range(NCORES)))

    outs = []
    for c in range(NCORES):
        yp = np.asarray(res.results[c]["yp"])
        ys = yp.reshape(NB, 2, 128, TT, 1024).transpose(0, 1, 3, 2, 4).reshape(NB, S, H)
        outs.append(ys)
    return np.concatenate(outs, axis=0).astype(np.float32)


# version 47
# speedup vs baseline: 1.0147x; 1.0147x over previous
import numpy as np
import ml_dtypes
import concourse.bass as bass
import concourse.mybir as mybir
from concourse.bass_utils import run_bass_kernel_spmd
from concourse.bass_core_rust import IndirectOffsetOnAxis

F32 = mybir.dt.float32
BF16 = mybir.dt.bfloat16
AF = mybir.ActivationFunctionType
ALU = mybir.AluOpType
AX = mybir.AxisListType

NCORES = 8
NB = 4
NPASS = NB * 2
DC = 8
TT = 8
NH = 16
E = 16
EPS = 1e-5
SM_BIAS = -3.0


class Ctr:
    def __init__(self, nc, name):
        self.h = nc.alloc_semaphore(name)
        self.n = 0


def build():
    nc = bass.Bass("TRN2", target_bir_lowering=False)

    xin = nc.dram_tensor("xp", [NB, 2, 128, TT * 1024], F32, kind="ExternalInput").ap()
    wqkvo = nc.dram_tensor("wqkvop", [4, 128, 8192], F32, kind="ExternalInput").ap()
    wg = nc.dram_tensor("wgp", [128, 128], F32, kind="ExternalInput").ap()
    w1 = nc.dram_tensor("w1p", [E, 128, 8192], F32, kind="ExternalInput").ap()
    w2 = nc.dram_tensor("w2p", [E, 128, 4096], F32, kind="ExternalInput").ap()
    ws1 = nc.dram_tensor("ws1p", [128, 16384], F32, kind="ExternalInput").ap()
    ws2 = nc.dram_tensor("ws2p", [128, 8192], F32, kind="ExternalInput").ap()
    identin = nc.dram_tensor("identc", [128, 128], BF16, kind="ExternalInput").ap()
    diagin = nc.dram_tensor("diagc", [128, 128], BF16, kind="ExternalInput").ap()
    ones64in = nc.dram_tensor("ones64c", [1, 64], BF16, kind="ExternalInput").ap()
    ultriin = nc.dram_tensor("ultric", [128, 128], BF16, kind="ExternalInput").ap()
    iota16in = nc.dram_tensor("iota16c", [128, 16], F32, kind="ExternalInput").ap()
    iota128in = nc.dram_tensor("iota128c", [128, 32], F32, kind="ExternalInput").ap()
    pos128in = nc.dram_tensor("pos128c", [128, 1], F32, kind="ExternalInput").ap()
    onesf1in = nc.dram_tensor("onesf1c", [1, 128], F32, kind="ExternalInput").ap()
    ones1b = nc.dram_tensor("ones1bc", [128, 1], BF16, kind="ExternalInput").ap()
    tokp16in = nc.dram_tensor("tokp16c", [128, 16], F32, kind="ExternalInput").ap()
    NBLK = 32
    NSLOT = NBLK * 128
    tokd = nc.dram_tensor("tokd", [NSLOT, 16], F32, kind="Internal").ap()
    pdram = nc.dram_tensor("pdram", [NSLOT, 1024], BF16, kind="Internal").ap()
    ln3d = nc.dram_tensor("ln3d", [1024, 1024], BF16, kind="Internal").ap()
    w1m = nc.dram_tensor("w1m", [E * 128, 8192], BF16, kind="Internal").ap()
    w2m = nc.dram_tensor("w2m", [E * 128, 4096], BF16, kind="Internal").ap()
    wqm = nc.dram_tensor("wqm", [4, 128, 8192], BF16, kind="Internal").ap()
    ws1m = nc.dram_tensor("ws1m", [128, 16384], BF16, kind="Internal").ap()
    ws2m = nc.dram_tensor("ws2m", [128, 8192], BF16, kind="Internal").ap()
    yout = nc.dram_tensor("yp", [NB, 2, 128, TT * 1024], F32, kind="ExternalOutput").ap()

    A = nc.alloc_sbuf_tensor
    xtok = A("xtok", [128, TT, 1024], F32)
    lnfm = A("lnfm", [128, DC, 1024], BF16)
    kfm = A("kfm", [128, DC, 1024], BF16)
    vaug = A("vaug", [128, TT, 1040], BF16)
    wbuf = A("wbuf", [128, 2, DC, 1024], BF16)
    w2buf = A("w2buf", [128, 4, 1024], BF16)
    wgb = A("wgb", [128, DC, 16], BF16)
    zb = A("zb", [128, TT, 1024], F32)
    lntok = A("lntok", [128, 2, 1024], BF16)
    t32 = A("t32", [128, 1024], F32)
    arena = A("arena", [128, 20480], BF16)
    aap = arena.ap()
    sgt = lntok.ap()

    def carve(off_bytes, c, f):
        return aap[:, off_bytes // 2:off_bytes // 2 + c * f].rearrange(
            "p (c f) -> p c f", c=c)

    qfm = carve(0, DC, 1024)
    ofm = carve(16384, DC, 1024)
    Eb = carve(32768, 2, 1024)
    Rb = carve(36864, 1, 1024)[0:64, 0, :]
    rb = carve(38912, 1, 1024)[0:1, 0, :]
    ws2b = carve(0, DC, 1024)
    ysp = carve(16384, DC, 1024)
    lnflat = lnfm.ap().rearrange("p c f -> p (c f)")
    xfm = lnflat[:, 0:2048].rearrange("p (b c f) -> p b c f", b=2, c=8)
    hpf = lnflat[:, 2048:2560]
    hpTs = lnflat[:, 7168:7680]
    lnf32 = lnfm.bitcast(F32)
    psc = lnflat[:, 3072:5120].rearrange("p (b f) -> p b f", b=2)
    xg2 = carve(32768, 2, 1024)
    arena32 = arena.bitcast(F32)
    gbufs = {(0, 0): aap[:, 16384:17408], (0, 1): aap[:, 17408:18432],
             (1, 0): aap[:, 18432:19456], (1, 1): aap[:, 19456:20480]}
    zb_bf_stage = zb.bitcast(BF16).ap().rearrange(
        "p t f -> p (t f)")

    identb = A("identb", [128, 128], BF16)
    ultrib = A("ultrib", [128, 128], BF16)
    iota16b = A("iota16b", [128, 16], F32)
    iota128b = A("iota128b", [128, 32], F32)
    pos128b = A("pos128b", [128, 1], F32)
    onesf1b = A("onesf1b", [1, 128], F32)
    ones1bb = A("ones1bb", [128, 1], BF16)
    indbf = A("indbf", [128, 16], BF16)
    slotf = A("slotf", [128, 16], F32)
    s16a = A("s16a", [128, 16], F32)
    sl2f = A("sl2f", [128, 2], F32)
    slot2i = A("slot2i", [128, TT, 2], mybir.dt.int32)
    prefb = A("prefb", [128, TT, 16], F32)
    selp = A("selp", [128, TT, 2], F32)
    cwvsb = A("cwvsb", [128, TT, 2], F32)
    cnts_sb = A("cnts_sb", [1, 128], F32)
    run_sb = A("run_sb", [1, 16], F32)
    tbase_sb = A("tbase_sb", [1, 128], F32)
    rbase_sb = A("rbase_sb", [1, 16], F32)
    r128i = A("r128i", [1, 16], mybir.dt.int32)
    r128f = A("r128f", [1, 16], F32)
    r128g = A("r128g", [1, 16], F32)
    r128h = A("r128h", [1, 16], F32)
    blkrow_sb = A("blkrow_sb", [1, 32], F32)
    rbaseb = A("rbaseb", [128, 16], F32)
    idxwf = A("idxwf", [128, NBLK], F32)
    idxwi = A("idxwi", [128, NBLK], mybir.dt.int32)
    tib = A("tib", [128, 2, 16], F32)
    tii = A("tii", [128, 2, 1], mybir.dt.int32)
    tidv = A("tidv", [128, 16], F32)
    tokp16b = A("tokp16b", [128, 16], F32)
    zt16 = A("zt16", [128, 16], F32)
    diagb = A("diagb", [128, 128], BF16)
    ones64b = A("ones64b", [1, 64], BF16)
    st12 = A("st12", [128, 24], F32)
    mvs = A("mvs", [128, 4], F32)
    aux = A("aux", [128, 24], F32)
    gex = A("gex", [128, 16], F32)
    gsc = A("gsc", [128, 16], F32)
    gtmp = A("gtmp", [128, 16], F32)
    cbias = A("cbias", [128, 1], F32)
    cscale = A("cscale", [128, 1], F32)
    ceps = A("ceps", [128, 1], F32)

    psA = nc.alloc_psum_tensor("psA", [128, 2, 1024], F32)
    psB = nc.alloc_psum_tensor("psB", [128, 2, 1024], F32)
    psB_bf = psB.bitcast(BF16)
    psA_bf = psA.bitcast(BF16)

    sW = Ctr(nc, "sW")
    sE = Ctr(nc, "sE")
    sX = Ctr(nc, "sX")
    sT = Ctr(nc, "sT")
    sS = Ctr(nc, "sS")
    sV = Ctr(nc, "sV")
    sBar = Ctr(nc, "sBar")

    EVN = ("te", "se", "ve", "po", "sy")
    events = {k: [] for k in EVN}

    def on(eng, fn):
        events[eng].append(fn)

    nbar = [0]

    def barrier():
        nbar[0] += 1
        target = 5 * nbar[0]
        for k in EVN:
            def f(eng, target=target):
                eng.sem_inc(sBar.h, 1)
                eng.wait_ge(sBar.h, target)
            on(k, f)

    def _waits(eng, waits):
        for (sem, val) in waits:
            if val > 0:
                eng.wait_ge(sem.h, val)

    def V(fn, waits=(), lp=None):
        pre = sV.n
        waits = tuple((s, v) for (s, v) in waits
                      if not (s is sV and v <= pre))

        def f(eng, fn=fn, waits=waits, pre=pre, lp=lp):
            eng.wait_ge(sV.h, pre)
            _waits(eng, waits)
            if lp:
                with nc.allow_low_precision(reason=lp):
                    ins = fn(eng)
            else:
                ins = fn(eng)
            ins.then_inc(sV.h, 1)
        on("ve", f)
        sV.n += 1
        return sV.n

    def S(fn, waits=()):
        pre = sS.n
        waits = tuple((s, v) for (s, v) in waits
                      if not (s is sS and v <= pre))

        def f(eng, fn=fn, waits=waits, pre=pre):
            eng.wait_ge(sS.h, pre)
            _waits(eng, waits)
            fn(eng).then_inc(sS.h, 1)
        on("se", f)
        sS.n += 1
        return sS.n

    def T(fn, waits=(), inc=True, lp=None):
        def f(eng, fn=fn, waits=tuple(waits), lp=lp, inc=inc):
            _waits(eng, waits)
            if lp:
                with nc.allow_low_precision(reason=lp):
                    ins = fn(eng)
            else:
                ins = fn(eng)
            if inc:
                ins.then_inc(sT.h, 1)
        on("te", f)
        if inc:
            sT.n += 1
            return sT.n

    def DMA_W(out, in_, waits=()):
        def f(eng, out=out, in_=in_, waits=tuple(waits)):
            _waits(eng, waits)
            eng.dma_start(out=out, in_=in_).then_inc(sW.h, 16)
        on("po", f)
        sW.n += 16
        return sW.n

    def DMA_X(out, in_, waits=()):
        def f(eng, out=out, in_=in_, waits=tuple(waits)):
            _waits(eng, waits)
            eng.dma_start(out=out, in_=in_).then_inc(sX.h, 16)
        on("sy", f)
        sX.n += 16
        return sX.n

    def DMA_XT(out, in_, waits=()):
        def f(eng, out=out, in_=in_, waits=tuple(waits)):
            _waits(eng, waits)
            eng.dma_start_transpose(out, in_).then_inc(sX.h, 16)
        on("sy", f)
        sX.n += 16
        return sX.n

    def DMA_SCAT(out, idx, in_, waits=()):
        def f(eng, out=out, idx=idx, in_=in_, waits=tuple(waits)):
            _waits(eng, waits)
            eng.indirect_dma_start(out, IndirectOffsetOnAxis(idx, 0), in_,
                                   None).then_inc(sW.h, 16)
        on("po", f)
        sW.n += 16
        return sW.n

    def DMA_SE(out, in_, waits=()):
        def f(eng, out=out, in_=in_, waits=tuple(waits)):
            _waits(eng, waits)
            eng.dma_start(out=out, in_=in_).then_inc(sE.h, 16)
        on("se", f)
        sE.n += 16
        return sE.n

    def DMA_G(out, in_, idx, waits=()):
        def f(eng, out=out, idx=idx, in_=in_, waits=tuple(waits)):
            _waits(eng, waits)
            eng.indirect_dma_start(out, None, in_,
                                   IndirectOffsetOnAxis(idx, 0)).then_inc(sW.h, 16)
        on("po", f)
        sW.n += 16
        return sW.n

    def DMA_GACC(out, in_, idx, waits=()):
        def f(eng, out=out, idx=idx, in_=in_, waits=tuple(waits)):
            _waits(eng, waits)
            eng.indirect_dma_start(out, None, in_,
                                   IndirectOffsetOnAxis(idx, 0),
                                   compute_op=ALU.add).then_inc(sW.h, 16)
        on("po", f)
        sW.n += 16
        return sW.n


    DMA_X(identb.ap()[:], identin[:])
    DMA_X(diagb.ap()[:], diagin[:])
    DMA_X(ones64b.ap()[:], ones64in[:])
    wg_ld = DMA_W(wgb.ap()[:], wg.rearrange("p (c e) -> p c e", e=16))
    DMA_X(ultrib.ap()[:], ultriin[:])
    DMA_X(iota16b.ap()[:], iota16in[:])
    DMA_X(iota128b.ap()[:], iota128in[:])
    DMA_X(pos128b.ap()[:], pos128in[:])
    DMA_X(onesf1b.ap()[:], onesf1in[:])
    cst_x = DMA_X(ones1bb.ap()[:], ones1b[:])
    cst_all = DMA_X(tokp16b.ap()[:], tokp16in[:])
    xld = {
        (0, 0): DMA_W(xtok.ap()[:, 0:4, :].rearrange("p t f -> p (t f)"),
                      xin[0, 0][:, 0:4096]),
        (0, 1): DMA_W(xtok.ap()[:, 4:8, :].rearrange("p t f -> p (t f)"),
                      xin[0, 0][:, 4096:8192]),
    }
    pref = {
        0: DMA_W(wbuf.ap()[:, 0], wqkvo[1].rearrange("p (c f) -> p c f", f=1024)),
        1: DMA_W(wbuf.ap()[:, 1], wqkvo[2].rearrange("p (c f) -> p c f", f=1024)),
    }
    mz16 = V(lambda v: v.memset(zt16.ap()[:], 0.0), waits=((sX, cst_all),))
    zi_mark = 0
    for k in range(NBLK):
        zi_mark = DMA_W(tokd[k * 128:(k + 1) * 128, :], zt16.ap()[:],
                        waits=((sV, mz16),) if k == 0 else ())
    mir_box = [0]

    def emit_mirror_build():
        units = []
        for e in range(E):
            units.append((w1[e][:, 0:4096], w1m[e * 128:(e + 1) * 128, 0:4096]))
            units.append((w1[e][:, 4096:8192],
                          w1m[e * 128:(e + 1) * 128, 4096:8192]))
            units.append((w2[e], w2m[e * 128:(e + 1) * 128, :]))
        ld_m = {}
        st_m = {}
        for u, (src_ap, dst_ap) in enumerate(units):
            c0 = (u % 2) * 4096
            ld_m[u] = DMA_W(zb_bf_stage[:, c0:c0 + 4096], src_ap,
                            waits=((sX, st_m.get(u - 2, 0)),))
            st_m[u] = DMA_X(dst_ap, zb_bf_stage[:, c0:c0 + 4096],
                            waits=((sW, ld_m[u]),))
        mir_box[0] = st_m[len(units) - 1]
    V(lambda v: v.memset(cbias.ap()[:], SM_BIAS))
    V(lambda v: v.memset(cscale.ap()[:], 0.125))
    V(lambda v: v.memset(ceps.ap()[:], EPS))

    ln_prev_act = {}

    def ln_stats_v(sl, src_ap, waits_v=()):
        wv = list(waits_v)
        if sl in ln_prev_act:
            wv.append((sS, ln_prev_act[sl]))
        V(lambda v: v.bn_stats(st12.ap()[:, 12 * sl:12 * sl + 6], src_ap[:, 0:512]),
          waits=wv)
        V(lambda v: v.bn_stats(st12.ap()[:, 12 * sl + 6:12 * sl + 12],
                               src_ap[:, 512:1024]), waits=waits_v)
        return V(lambda v: v.bn_aggr(mvs.ap()[:, 2 * sl:2 * sl + 2],
                                     st12.ap()[:, 12 * sl:12 * sl + 12]))

    def ln_rstd(sl, mstats):
        a0 = 8 * sl
        mL = S(lambda s: s.activation(aux.ap()[:, a0:a0 + 1],
                                      mvs.ap()[:, 2 * sl + 1:2 * sl + 2], AF.Ln,
                                      bias=ceps.ap()[:, 0:1]), waits=((sV, mstats),))
        ln_prev_act[sl] = mL
        mS2 = S(lambda s: s.activation(aux.ap()[:, a0 + 1:a0 + 2],
                                       aux.ap()[:, a0:a0 + 1], AF.Exp, scale=-0.5))
        V(lambda v: v.tensor_scalar(aux.ap()[:, a0 + 2:a0 + 3],
                                    mvs.ap()[:, 2 * sl:2 * sl + 1],
                                    aux.ap()[:, a0 + 1:a0 + 2], -1.0,
                                    ALU.mult, ALU.mult), waits=((sS, mS2),))
        return V(lambda v: v.tensor_scalar(aux.ap()[:, a0 + 3:a0 + 4],
                                           mvs.ap()[:, 2 * sl:2 * sl + 1], -1.0,
                                           None, ALU.mult))

    def gate_tile(qt, gate_first_waits, pre_war):
        def gmm(te, qt=qt):
            for dcc in range(DC):
                ins = te.matmul(
                    psA.ap()[:, qt % 2, 0:16],
                    lnfm.ap()[:, dcc, qt * 128:(qt + 1) * 128],
                    wgb.ap()[:, dcc, :],
                    start=(dcc == 0), stop=(dcc == DC - 1))
            return ins
        mt = T(gmm, waits=gate_first_waits)
        V(lambda v, qt=qt: v.reduce_max(aux.ap()[:, 16:17], psA.ap()[:, qt % 2, 0:16],
                                        axis=AX.X), waits=((sT, mt),))
        V(lambda v: v.tensor_scalar(aux.ap()[:, 17:18], aux.ap()[:, 16:17], -1.0,
                                    None, ALU.mult))
        mex = S(lambda s, qt=qt: s.activation(
            gex.ap()[:], psA.ap()[:, qt % 2, 0:16], AF.Exp, bias=aux.ap()[:, 17:18]),
            waits=((sV, sV.n),))
        V(lambda v: v.reduce_sum(aux.ap()[:, 18:19], gex.ap()[:], axis=AX.X),
          waits=((sS, mex),))
        V(lambda v: v.reciprocal(aux.ap()[:, 19:20], aux.ap()[:, 18:19]))
        V(lambda v: v.tensor_scalar(gsc.ap()[:], gex.ap()[:], aux.ap()[:, 19:20],
                                    None, ALU.mult))
        V(lambda v: v.tensor_scalar(gtmp.ap()[:], iota16b.ap()[:], -1e-6,
                                    None, ALU.mult))
        V(lambda v: v.tensor_add(gsc.ap()[:], gsc.ap()[:], gtmp.ap()[:]))
        V(lambda v: v.reduce_max(aux.ap()[:, 16:17], gsc.ap()[:], axis=AX.X))
        V(lambda v: v.tensor_scalar(gtmp.ap()[:], gsc.ap()[:], aux.ap()[:, 16:17],
                                    None, ALU.is_equal))
        V(lambda v: v.scalar_tensor_tensor(gtmp.ap()[:], gtmp.ap()[:], -1e30,
                                           gsc.ap()[:], ALU.mult, ALU.add))
        V(lambda v: v.reduce_max(aux.ap()[:, 16:17], gtmp.ap()[:], axis=AX.X))
        V(lambda v: v.tensor_scalar(gtmp.ap()[:], gsc.ap()[:], aux.ap()[:, 16:17],
                                    None, ALU.is_ge))
        mind = V(lambda v: v.tensor_copy(indbf.ap()[:], gtmp.ap()[:]))

        def pmm(te, qt=qt):
            return te.matmul(psA.ap()[:, qt % 2, 32:48], ultrib.ap()[:],
                             indbf.ap()[:], start=True, stop=True)
        mpre = T(pmm, waits=[(sV, mind), (sS, mex), (sX, cst_x), (sV, pre_war)])

        def cmm(te, qt=qt):
            return te.matmul(psA.ap()[0:1, qt % 2, 512:528], ones1bb.ap()[:],
                             indbf.ap()[:], start=True, stop=True)
        mcnt = T(cmm, waits=((sV, mind),))
        V(lambda v, qt=qt: v.tensor_copy(prefb.ap()[:, qt, :],
                                         psA.ap()[:, qt % 2, 32:48]),
          waits=((sT, mpre),))
        V(lambda v, qt=qt: v.tensor_copy(cnts_sb.ap()[0:1, 16 * qt:16 * qt + 16],
                                         psA.ap()[0:1, qt % 2, 512:528]),
          waits=((sT, mcnt),))
        V(lambda v: v.reduce_max(aux.ap()[:, 20:21], gsc.ap()[:], axis=AX.X))
        V(lambda v: v.tensor_scalar(gex.ap()[:], gsc.ap()[:], aux.ap()[:, 20:21],
                                    None, ALU.is_equal))
        V(lambda v: v.tensor_mul(s16a.ap()[:], iota16b.ap()[:], gex.ap()[:]))
        V(lambda v, qt=qt: v.reduce_sum(selp.ap()[:, qt, 0:1], s16a.ap()[:],
                                        axis=AX.X))
        V(lambda v: v.tensor_sub(gtmp.ap()[:], gtmp.ap()[:], gex.ap()[:]))
        V(lambda v: v.tensor_mul(s16a.ap()[:], iota16b.ap()[:], gtmp.ap()[:]))
        V(lambda v, qt=qt: v.reduce_sum(selp.ap()[:, qt, 1:2], s16a.ap()[:],
                                        axis=AX.X))
        V(lambda v, qt=qt: v.tensor_copy(cwvsb.ap()[:, qt, 0:1],
                                         aux.ap()[:, 20:21]))
        mrt = V(lambda v, qt=qt: v.tensor_copy(cwvsb.ap()[:, qt, 1:2],
                                               aux.ap()[:, 16:17]))
        return mex, mrt

    def ln_to_fm(xwaits, with_gate=False, gate_deps=None):
        tp_mark = {}
        cp_mark = {}
        mex = {}
        mrt = {}
        mstats = {}
        mnorm_mark = {}
        lnd_mark = {}

        def chain(tt):
            sl = tt % 2
            a0 = 8 * sl
            mst = ln_rstd(sl, mstats[tt])
            nwaits = [(sV, mst)]
            if tt >= 2:
                nwaits.append((sT, tp_mark[tt - 2]))
                if lnd_mark.get(tt - 2):
                    nwaits.append((sX, lnd_mark[tt - 2]))
            elif gate_deps is not None:
                nwaits.extend(gate_deps.get("lnw01", ()))
            mnorm = S(lambda s, tt=tt, a0=a0, sl=sl: s.activation(
                lntok.ap()[:, sl, :], xtok.ap()[:, tt, :], AF.Identity,
                bias=aux.ap()[:, a0 + 2:a0 + 3], scale=aux.ap()[:, a0 + 1:a0 + 2]),
                waits=nwaits)
            mnorm_mark[tt] = mnorm
            if with_gate:
                lw = [(sS, mnorm)]
                if tt == 0 and gate_deps is not None:
                    lw.extend(gate_deps.get("lnd0", ()))
                lnd_mark[tt] = DMA_X(ln3d[tt * 128:(tt + 1) * 128, :],
                                     lntok.ap()[:, tt % 2, :], waits=tuple(lw))
            twaits = [(sS, mnorm)]
            if tt >= 2:
                twaits.append((sV, cp_mark[tt - 2]))
            elif gate_deps is not None and gate_deps.get("psb_war"):
                twaits.append((sV, gate_deps["psb_war"]))
            if gate_deps is not None:
                twaits.extend(gate_deps.get("te0", ()))

            def tp(te, tt=tt, sl=sl):
                for dcc in range(DC):
                    ins = te.transpose(psB_bf[:, sl, dcc * 128:(dcc + 1) * 128],
                                       lntok.ap()[:, sl, dcc * 128:(dcc + 1) * 128],
                                       identb.ap()[:])
                return ins
            tp_mark[tt] = T(tp, waits=twaits, lp="bf16 transpose")
            cwaits = [(sT, tp_mark[tt])]
            if tt < 2 and gate_deps is not None:
                cwaits.extend(gate_deps.get("cp0", ()))
            cp_mark[tt] = V(lambda v, tt=tt, sl=sl: v.tensor_copy(
                lnfm.ap()[:, :, tt * 128:(tt + 1) * 128],
                psB_bf[:, sl, 0:1024].rearrange("p (c q) -> p c q", c=DC)),
                waits=tuple(cwaits))

        def gate_emit(qt):
            gw = [(sV, cp_mark[qt]), (sW, wg_ld)]
            if qt >= 2:
                gw.append((sS, mex[qt - 2]))
                gw.append((sV, mrt[qt - 2]))
            elif gate_deps is not None and gate_deps.get("psa_war"):
                gw.append((sS, gate_deps["psa_war"]))
            mex[qt], mrt[qt] = gate_tile(qt, gw, mrt.get(qt - 2, 0))

        for tt in range(TT):
            mstats[tt] = ln_stats_v(tt % 2, xtok.ap()[:, tt, :],
                                    waits_v=xwaits.get(tt, ()))
            if tt >= 1:
                chain(tt - 1)
                if with_gate and tt >= 2:
                    gate_emit(tt - 2)
        chain(TT - 1)
        if with_gate:
            gate_emit(TT - 2)
            gate_emit(TT - 1)
        return tp_mark, cp_mark, mex, mrt, mnorm_mark, lnd_mark

    car_lnw01 = ()
    car_cp0 = ()
    car_ln3d = ()
    car_tok = ()
    car_tid = ()
    car_idx = ()
    car_pd2 = 0
    car_sil = 0
    car_tt = 0
    car_sg = {0: 0, 1: 0}
    car_w2c15 = 0

    for p in range(NPASS):
        b, w = p // 2, p % 2

        xw = {0: ((sW, xld[(p, 0)]),), 4: ((sW, xld[(p, 1)]),)}
        tp1, cp1, _, _, _, _ = ln_to_fm(
            xw, gate_deps={"lnw01": car_lnw01, "cp0": car_cp0,
                           "te0": ((sX, cst_all),) if p == 0 else ()})

        rot = [psA.ap()[:, 0], psA.ap()[:, 1], psB.ap()[:, 0], psB.ap()[:, 1]]
        nrot = [0]
        evac_mark = {}

        def fm_proj_group(oc, lhs_half, extra_waits):
            slot = nrot[0] % 4
            nrot[0] += 1
            waits = list(extra_waits)
            if slot in evac_mark:
                waits.append(evac_mark[slot])

            def mms(te, oc=oc, slot=slot, lhs_half=lhs_half):
                for dcc in range(DC):
                    for h2 in range(2):
                        ins = te.matmul(
                            rot[slot][:, h2 * 512:(h2 + 1) * 512],
                            wbuf.ap()[:, lhs_half, dcc, oc * 128:(oc + 1) * 128],
                            lnfm.ap()[:, dcc, h2 * 512:(h2 + 1) * 512],
                            start=(dcc == 0), stop=(dcc == DC - 1))
                return ins
            return slot, T(mms, waits=waits)

        lnfm_all = (sV, cp1[TT - 1])
        if w == 0:
            for oc in range(DC):
                ew = [(sW, pref[0]), lnfm_all] if oc == 0 else [(sW, pref[0])]
                slot, mt = fm_proj_group(oc, 0, ew)
                kw = [(sT, mt)]
                evac_mark[slot] = (sV, V(lambda v, oc=oc, slot=slot: v.tensor_copy(
                    kfm.ap()[:, oc, :], rot[slot]), waits=kw,
                    lp="k evac bf16"))
            k_done = sT.n
            mq = DMA_W(wbuf.ap()[:, 0], wqkvo[0].rearrange("p (c f) -> p c f", f=1024),
                       waits=((sT, k_done),))
            V(lambda v: v.memset(
                vaug.ap()[:].rearrange("p t (h c) -> p t h c", c=65)[:, :, :, 64:65], 1.0))
            for kt in range(TT):
                slot = nrot[0] % 4
                nrot[0] += 1
                waits = [(sW, pref[1])]
                if slot in evac_mark:
                    waits.append(evac_mark[slot])

                def mms(te, kt=kt, slot=slot):
                    for dcc in range(DC):
                        for h2 in range(2):
                            ins = te.matmul(
                                rot[slot][:, h2 * 512:(h2 + 1) * 512],
                                lnfm.ap()[:, dcc, kt * 128:(kt + 1) * 128],
                                wbuf.ap()[:, 1, dcc, h2 * 512:(h2 + 1) * 512],
                                start=(dcc == 0), stop=(dcc == DC - 1))
                    return ins
                mt = T(mms, waits=waits)
                evac_mark[slot] = (sV, V(lambda v, kt=kt, slot=slot: v.tensor_copy(
                    vaug.ap()[:, kt, :].rearrange("p (h c) -> p h c", c=65)[:, :, 0:64],
                    rot[slot].rearrange("p (h d) -> p h d", d=64)),
                    waits=((sT, mt),), lp="v to bf16"))
            v_done = sT.n
            mo = DMA_W(wbuf.ap()[:, 1], wqkvo[3].rearrange("p (c f) -> p c f", f=1024),
                       waits=((sT, v_done),))
            for oc in range(DC):
                slot, mt = fm_proj_group(oc, 0, [(sW, mq)])
                evac_mark[slot] = (sV, V(lambda v, oc=oc, slot=slot: v.tensor_copy(
                    qfm[:, oc, :], rot[slot]), waits=((sT, mt),),
                    lp="q evac bf16"))
            vaug_ready = sV.n
        else:
            mo = pref[1]
            for oc in range(DC):
                ew = [(sW, pref[0]), lnfm_all] if oc == 0 else [(sW, pref[0])]
                slot, mt = fm_proj_group(oc, 0, ew)
                evac_mark[slot] = (sV, V(lambda v, oc=oc, slot=slot: v.tensor_copy(
                    qfm[:, oc, :], rot[slot]), waits=((sT, mt),),
                    lp="q evac bf16"))
            vaug_ready = 0
        q_ev_last = sV.n
        p2_done = sT.n
        if p == 0:
            emit_mirror_build()
        mir_done = mir_box[0]

        e_prev = None
        a_prev_mark = None
        mRc_prev = 0
        monorm_prev = 0
        for h in range(NH):
            hc, hr = h // 2, (h % 2) * 64
            oslot = h % 2
            e_mark = {}
            a_mark = {}
            m_marks = {}

            def emit_av(kt, h=h, oslot=oslot):
                q0 = 128 * kt
                avw = [(sV, m_marks[kt])]
                if kt == 0:
                    if h == 0 and w == 0:
                        avw.append((sV, vaug_ready))
                    if h > 0:
                        avw.append((sV, mRc_prev))

                def avmm(te, kt=kt, q0=q0, h=h, oslot=oslot):
                    for c0 in (0, 512):
                        lo, hi = max(q0, c0), c0 + 512
                        if lo >= hi:
                            continue
                        ins = te.matmul(
                            psB.ap()[0:65, oslot, lo:hi],
                            vaug.ap()[:, kt, h * 65:h * 65 + 65],
                            Eb[:, kt % 2, lo - q0:hi - q0],
                            start=(kt == 0),
                            stop=(kt == 3 if c0 == 0 else kt == TT - 1))
                    return ins
                a_mark[kt] = T(avmm, waits=avw)

            for kt in range(TT):
                q0 = 128 * kt
                swaits = []
                if h == 0 and kt == 0:
                    swaits.append((sV, q_ev_last))
                if kt >= 2:
                    swaits.append((sS, e_mark[kt - 2]))
                elif e_prev is not None:
                    swaits.append((sS, e_prev[(6, 7)[kt]]))

                def smm(te, kt=kt, q0=q0, hc=hc, hr=hr):
                    for c0 in (0, 512):
                        lo, hi = max(c0, q0), c0 + 512
                        if lo >= hi:
                            continue
                        ins = te.matmul(
                            psA.ap()[:, kt % 2, lo:hi],
                            kfm.ap()[hr:hr + 64, hc, q0:q0 + 128],
                            qfm[hr:hr + 64, hc, lo:hi],
                            start=True, stop=True)
                    return ins
                s_mark = T(smm, waits=swaits)
                ewaits = [(sT, s_mark)]
                if kt >= 2:
                    ewaits.append((sT, a_mark[kt - 2]))
                elif a_prev_mark is not None:
                    ewaits.append((sT, a_prev_mark[(6, 7)[kt]]))
                e_mark[kt] = S(lambda s, kt=kt, q0=q0: s.activation(
                    Eb[:, kt % 2, 0:1024 - q0], psA.ap()[:, kt % 2, q0:1024],
                    AF.Exp, bias=cbias.ap()[:, 0:1], scale=cscale.ap()[:, 0:1]),
                    waits=ewaits)
                m_marks[kt] = V(lambda v, kt=kt: v.tensor_tensor(
                    Eb[:, kt % 2, 0:128], Eb[:, kt % 2, 0:128], diagb.ap()[:], ALU.mult),
                    waits=((sS, e_mark[kt]),))
                if kt >= 1:
                    emit_av(kt - 1)
            emit_av(TT - 1)
            monorm = 0
            for c0, amk in ((0, a_mark[3]), (512, a_mark[TT - 1])):
                S(lambda s, oslot=oslot, c0=c0: s.activation(
                    t32.ap()[0:1, c0:c0 + 512],
                    psB.ap()[64:65, oslot, c0:c0 + 512], AF.Ln),
                    waits=((sT, amk),))
                mrec = S(lambda s, c0=c0: s.activation(
                    rb[:, c0:c0 + 512], t32.ap()[0:1, c0:c0 + 512], AF.Exp,
                    scale=-1.0))

                def rmm(te, oslot=oslot, c0=c0):
                    return te.matmul(psB.ap()[0:64, 1 - oslot, c0:c0 + 512],
                                     ones64b.ap()[:], rb[:, c0:c0 + 512],
                                     start=True, stop=True)
                mr = T(rmm, waits=[(sS, mrec)]
                       + ([(sV, monorm_prev)] if h > 0 and c0 == 0 else []))
                mRc = V(lambda v, oslot=oslot, c0=c0: v.tensor_copy(
                    Rb[:, c0:c0 + 512], psB.ap()[0:64, 1 - oslot, c0:c0 + 512]),
                    waits=((sT, mr),), lp="R to bf16")
                monorm = V(lambda v, hc=hc, hr=hr, oslot=oslot, c0=c0:
                           v.tensor_tensor(
                               ofm[hr:hr + 64, hc, c0:c0 + 512],
                               psB.ap()[0:64, oslot, c0:c0 + 512],
                               Rb[:, c0:c0 + 512], ALU.mult),
                           waits=((sV, mRc),), lp="attn out to bf16")
            a_prev_mark = a_mark
            e_prev = e_mark
            mRc_prev = mRc
            monorm_prev = monorm
        p3_done = sT.n
        monorm_last = monorm_prev

        ev2 = {}
        p4_stats = {}
        mact4_box = [0]

        def p4_chain(qt):
            sl = qt % 2
            a0 = 8 * sl
            mr4 = ln_rstd(sl, p4_stats[qt])
            mact4_box[0] = S(lambda s, sl=sl, a0=a0: s.activation(
                t32.ap()[:], psA.ap()[:, sl], AF.Identity,
                bias=aux.ap()[:, a0 + 3:a0 + 4]),
                waits=((sV, mr4), (sV, ev2.get(1 - sl, 0))))
            ev2[sl] = V(lambda v, qt=qt, a0=a0: v.scalar_tensor_tensor(
                xtok.ap()[:, qt, :], t32.ap()[:], aux.ap()[:, a0 + 1:a0 + 2],
                xtok.ap()[:, qt, :], ALU.mult, ALU.add),
                waits=((sS, mact4_box[0]),))

        for qt in range(TT):
            slot = qt % 2
            waits = [(sW, mo)]
            if qt == 0:
                waits.append((sV, monorm_last))
            if slot in ev2:
                waits.append((sV, ev2[slot]))

            def womm(te, qt=qt, slot=slot):
                for dcc in range(DC):
                    for h2 in range(2):
                        ins = te.matmul(
                            psA.ap()[:, slot, h2 * 512:(h2 + 1) * 512],
                            ofm[:, dcc, qt * 128:(qt + 1) * 128],
                            wbuf.ap()[:, 1, dcc, h2 * 512:(h2 + 1) * 512],
                            start=(dcc == 0), stop=(dcc == DC - 1))
                return ins
            mt = T(womm, waits=waits)
            p4_stats[qt] = ln_stats_v(slot, psA.ap()[:, slot], waits_v=((sT, mt),))
            if qt >= 1:
                p4_chain(qt - 1)
        p4_chain(TT - 1)
        mact4 = mact4_box[0]
        wo_done = sT.n

        mws1a = DMA_W(wbuf.ap()[:, 0],
                      ws1.rearrange("p (c s) -> p c s", s=2048)[:, :, 0:1024],
                      waits=((sT, p2_done),))
        mws1b = DMA_W(wbuf.ap()[:, 1],
                      ws1.rearrange("p (c s) -> p c s", s=2048)[:, :, 1024:2048],
                      waits=((sT, wo_done),))

        car_lnw01_p5 = ((sT, tp1[TT - 1]),)
        mrun0 = V(lambda v: v.memset(run_sb.ap()[:], 0.0))
        tp5, cp5, mex, mrt5, mnorm5, lnd5 = ln_to_fm(
            {}, with_gate=True,
            gate_deps={"psb_war": monorm_last, "psa_war": mact4,
                       "lnw01": car_lnw01_p5, "lnd0": car_ln3d})

        mws2 = DMA_W(ws2b[:, :, :], ws2.rearrange("p (c f) -> p c f", f=1024),
                     waits=((sT, p3_done),))
        sg_cons = {0: car_sg[0], 1: car_sg[1]}
        last_sil = car_sil
        last_tt = car_tt
        ys_g_done = 0
        ys_u_done = 0
        for i in range(TT):
            def gmms6(te, i=i):
                off = i * 128
                for dcc in range(DC):
                    for h2 in range(2):
                        ins = te.matmul(
                            psA.ap()[:, 0, h2 * 512:(h2 + 1) * 512],
                            wbuf.ap()[:, 1, dcc, off:off + 128],
                            lnfm.ap()[:, dcc, h2 * 512:(h2 + 1) * 512],
                            start=(dcc == 0), stop=(dcc == DC - 1))
                return ins
            gw6 = [(sW, mws1b), (sS, last_sil)]
            if i == 0:
                gw6.append((sS, mex[6]))
                gw6.append((sV, mrt5[6]))
            ys_g_done = T(gmms6, waits=gw6)
            silw6 = [(sT, ys_g_done), (sV, sg_cons.get(i % 2, 0))]
            if i < 2:
                silw6.append((sX, lnd5[(6, 7)[i]]))
            last_sil = S(lambda s, i=i: s.activation(
                sgt[:, i % 2, :], psA.ap()[:, 0], AF.Silu), waits=silw6)

            def umms6(te, i=i):
                off = i * 128
                for dcc in range(DC):
                    for h2 in range(2):
                        ins = te.matmul(
                            psA.ap()[:, 1, h2 * 512:(h2 + 1) * 512],
                            wbuf.ap()[:, 0, dcc, off:off + 128],
                            lnfm.ap()[:, dcc, h2 * 512:(h2 + 1) * 512],
                            start=(dcc == 0), stop=(dcc == DC - 1))
                return ins
            uw6 = [(sW, mws1a), (sV, last_tt)]
            if i == 0:
                uw6.append((sS, mex[7]))
                uw6.append((sV, mrt5[7]))
            ys_u_done = T(umms6, waits=uw6)
            last_tt = V(lambda v, i=i: v.tensor_tensor(
                ysp[:, i, :], psA.ap()[:, 1], sgt[:, i % 2, :], ALU.mult),
                waits=((sT, ys_u_done), (sS, last_sil)), lp="ys_pre to bf16")
            sg_cons[i % 2] = last_tt
        ysp_done = last_tt

        for qt in range(TT):
            V(lambda v, qt=qt: v.tensor_copy(
                tbase_sb.ap()[0:1, 16 * qt:16 * qt + 16], run_sb.ap()[:]),
              waits=((sV, mrt5[qt]), (sV, mrun0)) if qt == 0 else ((sV, mrt5[qt]),))
            V(lambda v, qt=qt: v.tensor_add(
                run_sb.ap()[:], run_sb.ap()[:],
                cnts_sb.ap()[0:1, 16 * qt:16 * qt + 16]))
        V(lambda v: v.tensor_scalar(r128g.ap()[:], run_sb.ap()[:], 127.0,
                                    1.0 / 128.0, ALU.add, ALU.mult))
        V(lambda v: v.tensor_copy(r128i.ap()[:], r128g.ap()[:]))
        V(lambda v: v.tensor_copy(r128f.ap()[:], r128i.ap()[:]))
        V(lambda v: v.tensor_tensor(r128h.ap()[:], r128f.ap()[:], r128g.ap()[:],
                                    ALU.is_gt))
        V(lambda v: v.tensor_sub(r128f.ap()[:], r128f.ap()[:], r128h.ap()[:]))
        V(lambda v: v.tensor_scalar(r128f.ap()[:], r128f.ap()[:], 128.0, None,
                                    ALU.mult))
        V(lambda v: v.memset(rbase_sb.ap()[0:1, 0:1], 0.0))
        for e2_ in range(1, 16):
            V(lambda v, e2_=e2_: v.tensor_add(
                rbase_sb.ap()[0:1, e2_:e2_ + 1], rbase_sb.ap()[0:1, e2_ - 1:e2_],
                r128f.ap()[0:1, e2_ - 1:e2_]))
        mrb = sV.n
        def rbmm(te):
            return te.matmul(psB.ap()[:, 0, 0:16], onesf1b.ap()[:],
                             rbase_sb.ap()[:], start=True, stop=True)
        mrbT = T(rbmm, waits=((sV, mrb), (sV, cp5[TT - 1])))
        mrbV = V(lambda v: v.tensor_copy(rbaseb.ap()[:], psB.ap()[:, 0, 0:16]),
                 waits=((sT, mrbT),))
        V(lambda v: v.tensor_scalar(s16a.ap()[:], rbaseb.ap()[:],
                                    pos128b.ap()[:, 0:1], None, ALU.is_le))
        V(lambda v: v.reduce_sum(sl2f.ap()[:, 0:1], s16a.ap()[:], axis=AX.X))
        V(lambda v: v.tensor_scalar(sl2f.ap()[:, 0:1], sl2f.ap()[:, 0:1], -1.0,
                                    None, ALU.add))
        V(lambda v: v.memset(indbf.ap()[:], 0.0))
        mcpd = V(lambda v: v.tensor_copy(indbf.ap()[:, 0:1], sl2f.ap()[:, 0:1]))

        def temm(te):
            return te.transpose(psB_bf[0:16, 0, 1024:1152], indbf.ap()[:],
                                identb.ap()[:])
        mtrT = T(temm, waits=((sV, mcpd),), lp="bf16 transpose")
        mblkrow = V(lambda v: v.tensor_copy(blkrow_sb.ap()[:],
                                            psB_bf[0:1, 0, 1024:1056]),
                    waits=((sT, mtrT),))
        for blk in range(NBLK):
            def bmm(te, blk=blk):
                return te.matmul(psB.ap()[:, 1, blk:blk + 1], onesf1b.ap()[:],
                                 blkrow_sb.ap()[0:1, blk:blk + 1],
                                 start=True, stop=True)
            T(bmm, waits=((sV, mblkrow),) if blk == 0 else ())
        midxT = sT.n
        V(lambda v: v.scalar_tensor_tensor(
            idxwf.ap()[:], psB.ap()[:, 1, 0:NBLK], 128.0, iota128b.ap()[:],
            ALU.mult, ALU.add), waits=((sT, midxT),) + car_idx)
        midxw = V(lambda v: v.tensor_copy(idxwi.ap()[:], idxwf.ap()[:]))
        scat_last = 0
        slot_mark = {}
        tid_mark = 0
        for qt in range(TT):
            def tbmm(te, qt=qt):
                return te.matmul(psB.ap()[:, 0, 16:32], onesf1b.ap()[:],
                                 tbase_sb.ap()[0:1, 16 * qt:16 * qt + 16],
                                 start=True, stop=True)
            mtb = T(tbmm, waits=((sV, slot_mark[qt - 1]),) if qt else
                    ((sV, mrbV),))
            V(lambda v, qt=qt: v.tensor_tensor(slotf.ap()[:],
                                               psB.ap()[:, 0, 16:32],
                                               rbaseb.ap()[:], ALU.add),
              waits=((sT, mtb),))
            V(lambda v, qt=qt: v.tensor_add(slotf.ap()[:], slotf.ap()[:],
                                            prefb.ap()[:, qt, :]))
            for k in (0, 1):
                V(lambda v, qt=qt, k=k: v.tensor_scalar(
                    s16a.ap()[:], iota16b.ap()[:], selp.ap()[:, qt, k:k + 1],
                    None, ALU.is_equal))
                V(lambda v: v.tensor_mul(s16a.ap()[:], slotf.ap()[:], s16a.ap()[:]))
                V(lambda v, k=k: v.reduce_sum(sl2f.ap()[:, k:k + 1], s16a.ap()[:],
                                              axis=AX.X))
            V(lambda v: v.tensor_scalar(sl2f.ap()[:], sl2f.ap()[:],
                                        float(NSLOT - 1), None, ALU.min))
            swts = [(sW, car_pd2)] if (qt == 0 and car_pd2) else []
            slot_mark[qt] = V(lambda v, qt=qt: v.tensor_copy(
                slot2i.ap()[:, qt, :], sl2f.ap()[:]), waits=swts)
            tid_mark = V(lambda v, qt=qt: v.tensor_scalar(
                tidv.ap()[:], tokp16b.ap()[:], float(qt * 128), None, ALU.add),
                waits=[(sW, scat_last)] + (list(car_tid) if qt == 0 else []))
            for k in (0, 1):
                sww = [(sV, slot_mark[qt]), (sV, tid_mark)]
                if qt == 0 and k == 0:
                    sww.extend(car_tok)
                scat_last = DMA_SCAT(tokd[:, :], slot2i.ap()[:, qt, k:k + 1],
                                     tidv.ap()[:], waits=sww)

        w1ldb = {}
        w2ldb = {}
        tibld = {}
        tcast = {}
        xgld = {}
        xgtr = {}
        xfmcp = {}
        gu_done = {}
        sil_m = {}
        hpTT = {}
        hptr = {}
        hpcp = {}
        w2mm = {}
        pts = {}
        pw = {}

        def emit_blk_loads(blk):
            w1w = [(sV, midxw)]
            if p == 0 and blk < 2:
                w1w.append((sX, mir_done))
            if blk < 2:
                w1w.append((sT, ys_u_done))
            else:
                w1w.append((sT, gu_done[blk - 2]))
            w1ldb[blk] = DMA_G(
                wbuf.ap()[:, blk % 2].rearrange("p c f -> p (c f)"),
                w1m[:, :], idxwi.ap()[:, blk:blk + 1], waits=w1w)
            tbw = [(sW, scat_last)]
            if blk >= 2:
                tbw.append((sV, tcast[blk - 2]))
            tibld[blk] = DMA_X(tib.ap()[:, blk % 2, :],
                               tokd[blk * 128:(blk + 1) * 128, :], waits=tbw)
            tcw = [(sX, tibld[blk])]
            if blk >= 2:
                tcw.append((sW, xgld[blk - 2]))
            tcast[blk] = V(lambda v, blk=blk: v.tensor_copy(
                tii.ap()[:, blk % 2, :], tib.ap()[:, blk % 2, 0:1]), waits=tcw)
            xgw = [(sV, tcast[blk])]
            if blk < 2:
                xgw.append((sX, lnd5[TT - 1]))
            else:
                xgw.append((sT, xgtr[blk - 2]))
            xgld[blk] = DMA_G(xg2[:, blk % 2, :], ln3d[:, :],
                              tii.ap()[:, blk % 2, 0:1], waits=xgw)

        emit_blk_loads(0)
        emit_blk_loads(1)
        w2ldb[0] = DMA_G(
            w2buf.ap()[:].rearrange("p c f -> p (c f)"), w2m[:, :],
            idxwi.ap()[:, 0:1],
            waits=[(sV, midxw)] + ([(sT, car_w2c15)] if car_w2c15 else []))
        for blk in range(NBLK):
            def xtr(te, blk=blk):
                for dcc in range(DC):
                    ins = te.transpose(
                        psA_bf[:, 1, 1024 + dcc * 128:1024 + (dcc + 1) * 128],
                        xg2[:, blk % 2, dcc * 128:(dcc + 1) * 128],
                        identb.ap()[:])
                return ins
            xw7 = [(sW, xgld[blk]), (sS, xfmcp.get(blk - 1, 0))]
            if blk < 2:
                xw7.append((sV, ysp_done))
            xgtr[blk] = T(xtr, waits=xw7, lp="bf16 transpose")
            fcw = [(sT, xgtr[blk])]
            if blk < 2:
                fcw.append((sT, ys_u_done))
            else:
                fcw.append((sT, gu_done[blk - 2]))
            xfmcp[blk] = S(lambda s, blk=blk: s.activation(
                xfm[:, blk % 2].rearrange("p c f -> p (c f)"),
                psA_bf[:, 1, 1024:2048], AF.Identity), waits=fcw)

            def gmm7(te, blk=blk):
                for dcc in range(DC):
                    ins = te.matmul(psA.ap()[:, 0, 512:1024],
                                    xfm[:, blk % 2, dcc, :],
                                    wbuf.ap()[:, blk % 2, dcc, 512:1024],
                                    start=(dcc == 0), stop=(dcc == DC - 1))
                return ins
            gw7 = [(sW, w1ldb[blk]), (sS, xfmcp[blk]),
                   (sS, sil_m.get(blk - 1, 0))]
            g_done = T(gmm7, waits=gw7)
            sil_m[blk] = S(lambda s, blk=blk: s.activation(
                sgt[:, blk % 2, 0:512], psA.ap()[:, 0, 512:1024], AF.Silu),
                waits=((sT, g_done), (sV, sg_cons[blk % 2])))

            def umm7(te, blk=blk):
                for dcc in range(DC):
                    ins = te.matmul(psA.ap()[:, 0, 0:512],
                                    xfm[:, blk % 2, dcc, :],
                                    wbuf.ap()[:, blk % 2, dcc, 0:512],
                                    start=(dcc == 0), stop=(dcc == DC - 1))
                return ins
            gu_done[blk] = T(umm7, waits=((sV, hpTT.get(blk - 1, 0)),))
            hpTT[blk] = V(lambda v, blk=blk: v.tensor_tensor(
                hpf[:, :], psA.ap()[:, 0, 0:512], sgt[:, blk % 2, 0:512],
                ALU.mult), waits=((sT, gu_done[blk]), (sS, sil_m[blk]),
                                  (sT, hptr.get(blk - 1, 0))), lp="h to bf16")
            sg_cons[blk % 2] = hpTT[blk]

            def htr(te, blk=blk):
                for hcc in range(4):
                    ins = te.transpose(
                        psA_bf[:, 1, hcc * 128:(hcc + 1) * 128],
                        hpf[:, hcc * 128:(hcc + 1) * 128], identb.ap()[:])
                return ins
            hptr[blk] = T(htr, waits=((sV, hpTT[blk]),
                                      (sS, hpcp.get(blk - 1, 0))),
                          lp="bf16 transpose")
            hpcp[blk] = S(lambda s, blk=blk: s.activation(
                hpTs[:, :], psA_bf[:, 1, 0:512], AF.Identity),
                waits=((sT, hptr[blk]), (sT, w2mm.get(blk - 1, 0))))

            def w2mms(te, blk=blk):
                for hcc in range(4):
                    for h2 in range(2):
                        ins = te.matmul(
                            psB.ap()[:, blk % 2, h2 * 512:(h2 + 1) * 512],
                            hpTs[:, hcc * 128:(hcc + 1) * 128],
                            w2buf.ap()[:, hcc, h2 * 512:(h2 + 1) * 512],
                            start=(hcc == 0), stop=(hcc == 3))
                return ins
            w2w7 = [(sW, w2ldb[blk]), (sS, hpcp[blk]),
                    (sV, pts.get(blk - 2, 0))]
            if blk < 2:
                w2w7.append((sV, slot_mark[TT - 1]))
            w2mm[blk] = T(w2mms, waits=w2w7)
            pts[blk] = V(lambda v, blk=blk: v.tensor_copy(
                psc[:, blk % 2, :], psB.ap()[:, blk % 2]),
                waits=((sT, w2mm[blk]), (sE, pw.get(blk - 2, 0))),
                lp="expert out staged bf16")
            pww = [(sV, pts[blk])]
            if blk == 0 and car_pd2:
                pww.append((sW, car_pd2))
            pw[blk] = DMA_SE(pdram[blk * 128:(blk + 1) * 128, :],
                             psc[:, blk % 2, :], waits=pww)
            if blk + 1 < NBLK:
                w2ldb[blk + 1] = DMA_G(
                    w2buf.ap()[:].rearrange("p c f -> p (c f)"), w2m[:, :],
                    idxwi.ap()[:, blk + 1:blk + 2],
                    waits=((sV, midxw), (sT, w2mm[blk])))
            if blk + 2 < NBLK:
                emit_blk_loads(blk + 2)
        pw_final = pw[NBLK - 1]
        car_ln3d = ((sW, xgld[NBLK - 1]),)
        car_tok = ((sX, tibld[NBLK - 1]),)
        car_idx = ((sW, w2ldb[NBLK - 1]),)
        car_w2c15 = w2mm[NBLK - 1]
        car_sil = hpcp[NBLK - 1]
        car_tt = hpTT[NBLK - 1]
        car_sg = {0: hpTT[NBLK - 2], 1: hpTT[NBLK - 1]}
        car_cp0 = ((sE, pw_final),)

        xst = {}
        out_marks = {}
        p8_stats = {}
        mact8_box = [0]

        def p8_chain(qt):
            sl = qt % 2
            a0 = 8 * sl
            mr8 = ln_rstd(sl, p8_stats[qt])
            mact8_box[0] = S(lambda s, qt=qt, a0=a0: s.activation(
                t32.ap()[:], zb.ap()[:, qt, :], AF.Identity,
                bias=aux.ap()[:, a0 + 3:a0 + 4]),
                waits=((sV, mr8), (sV, xst.get(qt - 1, 0))))
            xst[qt] = V(lambda v, qt=qt, a0=a0: v.scalar_tensor_tensor(
                xtok.ap()[:, qt, :], t32.ap()[:], aux.ap()[:, a0 + 1:a0 + 2],
                xtok.ap()[:, qt, :], ALU.mult, ALU.add),
                waits=((sS, mact8_box[0]),))
            out_marks[qt] = DMA_X(yout[b, w][:, qt * 1024:(qt + 1) * 1024],
                                  xtok.ap()[:, qt, :], waits=((sV, xst[qt]),))

        gth = {}
        stt = {}
        for qt in range(TT):
            slot = qt % 2
            wswaits = [(sW, mws2), (sV, ysp_done)]
            if qt >= 2:
                wswaits.append((sV, xst[qt - 2]))
            else:
                wswaits.append((sV, pts[NBLK - 2 + qt]))

            def ws2mms(te, qt=qt, slot=slot):
                for hcc in range(DC):
                    for h2 in range(2):
                        ins = te.matmul(
                            psB.ap()[:, slot, h2 * 512:(h2 + 1) * 512],
                            ysp[:, hcc, qt * 128:(qt + 1) * 128],
                            ws2b[:, hcc, h2 * 512:(h2 + 1) * 512],
                            start=(hcc == 0), stop=(hcc == DC - 1))
                return ins
            mt = T(ws2mms, waits=wswaits)
            if qt >= 1:
                p8_chain(qt - 1)
            ziw = [(sT, mt)]
            if p == 0 and qt < 2:
                ziw.append((sX, mir_done))
            zini = V(lambda v, qt=qt, slot=slot: v.tensor_copy(
                zb.ap()[:, qt, :], psB.ap()[:, slot]), waits=ziw)
            for k in (0, 1):
                gbk = gbufs[(qt % 2, k)]
                gw8 = [(sV, stt.get((qt - 2, k), 0))]
                if qt == 0 and k == 0:
                    gw8.append((sE, pw_final))
                gth[(qt, k)] = DMA_G(gbk, pdram[:, :],
                                     slot2i.ap()[:, qt, k:k + 1], waits=gw8)
                stt[(qt, k)] = V(
                    lambda v, qt=qt, gbk=gbk, k=k: v.scalar_tensor_tensor(
                        zb.ap()[:, qt, :], gbk, cwvsb.ap()[:, qt, k:k + 1],
                        zb.ap()[:, qt, :], ALU.mult, ALU.add),
                    waits=((sW, gth[(qt, k)]),))
            p8_stats[qt] = ln_stats_v(slot, zb.ap()[:, qt, :],
                                      waits_v=((sV, stt[(qt, 1)]),))
        p8_chain(TT - 1)
        car_pd2 = gth[(TT - 1, 1)]
        p8_u_done = sT.n

        if p + 1 < NPASS:
            nw = (p + 1) % 2
            idx0, idx1 = (1, 2) if nw == 0 else (0, 3)
            pref = {
                0: DMA_W(wbuf.ap()[:, 0],
                         wqkvo[idx0].rearrange("p (c f) -> p c f", f=1024),
                         waits=((sT, p8_u_done),)),
                1: DMA_W(wbuf.ap()[:, 1],
                         wqkvo[idx1].rearrange("p (c f) -> p c f", f=1024),
                         waits=((sT, p8_u_done),)),
            }
            xld[(p + 1, 0)] = DMA_W(
                xtok.ap()[:, 0:4, :].rearrange("p t f -> p (t f)"),
                xin[(p + 1) // 2, nw][:, 0:4096],
                waits=((sX, out_marks[3]),))
            xld[(p + 1, 1)] = DMA_W(
                xtok.ap()[:, 4:8, :].rearrange("p t f -> p (t f)"),
                xin[(p + 1) // 2, nw][:, 4096:8192],
                waits=((sX, out_marks[7]),))
        car_lnw01 = ((sX, lnd5[TT - 1]),)
        car_tid = ((sW, scat_last),)
        barrier()

    fin = sX.n
    on("sy", lambda eng: eng.wait_ge(sX.h, fin))

    with nc.Block() as block:
        @block.tensor
        def _(te):
            for f in events["te"]:
                f(te)

        @block.scalar
        def _(se):
            for f in events["se"]:
                f(se)

        @block.vector
        def _(ve):
            for f in events["ve"]:
                f(ve)

        @block.gpsimd
        def _(po):
            for f in events["po"]:
                f(po)

        @block.sync
        def _(sy):
            for f in events["sy"]:
                f(sy)

    return nc


_NC_CACHE = [None]


def kernel(L0_attn_hidden_states, L0_attn_Wq, L0_attn_Wk, L0_attn_Wv, L0_attn_Wo,
           L0_ffn_Wg, L0_ffn_W1, L0_ffn_W2, L0_ffn_Ws1, L0_ffn_Ws2):
    x = np.asarray(L0_attn_hidden_states)
    B, S, H = x.shape

    def pack_w(wt):
        d, f = wt.shape
        return np.ascontiguousarray(
            wt.reshape(d // 128, 128, f).transpose(1, 0, 2).reshape(128, (d // 128) * f))

    wqkvop = np.stack([pack_w(np.asarray(w).T.copy())
                       for w in (L0_attn_Wq, L0_attn_Wk, L0_attn_Wv, L0_attn_Wo)])
    wgp = pack_w(np.asarray(L0_ffn_Wg).T.copy())
    w1p = np.stack([pack_w(np.asarray(L0_ffn_W1)[e]) for e in range(E)])
    w2p = np.stack([pack_w(np.asarray(L0_ffn_W2)[e]) for e in range(E)])
    ws1p = pack_w(np.asarray(L0_ffn_Ws1))
    ws2p = pack_w(np.asarray(L0_ffn_Ws2))

    identc = np.eye(128, dtype=ml_dtypes.bfloat16)
    diagc = np.triu(np.ones((128, 128))).astype(ml_dtypes.bfloat16)
    ones64c = np.ones((1, 64), dtype=ml_dtypes.bfloat16)
    ultric = np.triu(np.ones((128, 128)), k=1).astype(ml_dtypes.bfloat16)
    iota16c = np.ascontiguousarray(
        np.broadcast_to(np.arange(16, dtype=np.float32), (128, 16)))
    iota128c = np.ascontiguousarray(np.broadcast_to(
        np.arange(128, dtype=np.float32).reshape(128, 1), (128, 32)))
    pos128c = (np.arange(128, dtype=np.float32) * 128.0).reshape(128, 1)
    onesf1c = np.ones((1, 128), dtype=np.float32)
    ones1bc = np.ones((128, 1), dtype=ml_dtypes.bfloat16)
    tokp16c = np.ascontiguousarray(np.broadcast_to(
        np.arange(128, dtype=np.float32).reshape(128, 1), (128, 16)))

    if _NC_CACHE[0] is None:
        _NC_CACHE[0] = build()
    nc = _NC_CACHE[0]

    in_maps = []
    for c in range(NCORES):
        xs = np.asarray(x[c * NB:(c + 1) * NB], dtype=np.float32)
        xp = np.ascontiguousarray(
            xs.reshape(NB, 2, TT, 128, 1024).transpose(0, 1, 3, 2, 4)
              .reshape(NB, 2, 128, TT * 1024))
        in_maps.append({
            "xp": xp, "wqkvop": wqkvop, "wgp": wgp, "w1p": w1p, "w2p": w2p,
            "ws1p": ws1p, "ws2p": ws2p, "identc": identc, "diagc": diagc,
            "ones64c": ones64c, "ultric": ultric, "iota16c": iota16c,
            "iota128c": iota128c, "pos128c": pos128c, "onesf1c": onesf1c,
            "ones1bc": ones1bc, "tokp16c": tokp16c,
        })

    res = run_bass_kernel_spmd(nc, in_maps, core_ids=list(range(NCORES)))

    outs = []
    for c in range(NCORES):
        yp = np.asarray(res.results[c]["yp"])
        ys = yp.reshape(NB, 2, 128, TT, 1024).transpose(0, 1, 3, 2, 4).reshape(NB, S, H)
        outs.append(ys)
    return np.concatenate(outs, axis=0).astype(np.float32)
